# version 18
# baseline (speedup 1.0000x reference)
import numpy as np
import concourse.bass as bass
import concourse.mybir as mybir
from concourse.bass_utils import run_bass_kernel_spmd

F32 = mybir.dt.float32
BF16 = mybir.dt.bfloat16
I32 = mybir.dt.int32
AT = mybir.ActivationFunctionType
OP = mybir.AluOpType
AX = mybir.AxisListType.X

ENGS = ("sync", "vector", "scalar", "tensor", "gpsimd")
MM_PARTS = ("strips", "mm", "reduce", "evac", "spill")
DBG_HBM = False
INTERLEAVE_HEAD = 3
XPROJ_IN_MG = False
WIN_PREFETCH = False
HEAD_MG_SCHED = {
    6: [(0, 0), (1, 1)],
    7: [(8, 0), (9, 1)],
    8: [(2, 0), (3, 1)],
    9: [(10, 0), (11, 1)],
    10: [(4, 0)],
    11: [(5, 1), (12, 0)],
    12: [(13, 1)],
    13: [(6, 0), (14, 1)],
    15: [(7, 0), (15, 1)],
}
DBG_HBM_SET = {"h_x1n", "h_ysave", "h_dbg_u", "h_dbg_xi", "h_dbg_xdbl",
               "h_up", "h_g2", "h_dsave", "h_wu", "h_dbg_xq"}
EPS = 1e-5
RQMAX = float(np.float32(1.0) / np.float32(32767.0))
R65536 = float(np.float32(1.0) / np.float32(65536.0))
DM, DI, DS, DTR = 1024, 2048, 16, 64
CM = DM // 128
CD = DI // 128
CI = 3072 // 128


class CFG:
    def __init__(self, lc=4096, t=256, gp_planes=0, ncores=8):
        self.LC = lc
        self.T = t
        self.NB = lc // t
        self.TT = t // 128
        self.GP = gp_planes
        self.NC = ncores


class Mark:
    __slots__ = ("sem", "val", "eng")

    def __init__(self, eng):
        self.eng = eng
        self.sem = None
        self.val = None


class Prog:

    def __init__(self, dma_pools):
        self.q = {e: [] for e in ENGS}
        self.lastm = {e: None for e in ENGS}
        self.wr = {}
        self.rd = {}
        self.marks = []
        self.pools = {}
        for name, n in dma_pools.items():
            self.pools[name] = {"n": n, "i": 0,
                                "last": [None] * n, "cnt": [0] * n}

    def _dep_wait(self, eng, mark):
        self.q[eng].append(("w", mark))

    def _collect(self, reads, writes):
        deps = []
        for r in reads:
            d = self.wr.get(r)
            if d is not None:
                deps.append(d)
        for w in writes:
            d = self.wr.get(w)
            if d is not None:
                deps.append(d)
            deps.extend(self.rd.get(w, ()))
        out, seen = [], set()
        for d in deps:
            if id(d) not in seen:
                seen.add(id(d))
                out.append(d)
        return out

    def op(self, eng, fn, reads=(), writes=(), dma=None):
        for d in self._collect(reads, writes):
            self._dep_wait(eng, d)
        m = Mark(eng)
        if dma is not None:
            pool = self.pools[dma]
            slot = pool["i"] % pool["n"]
            pool["i"] += 1
            sem = "%s%d" % (dma, slot)
            prev = pool["last"][slot]
            if prev is not None:
                self._dep_wait(eng, prev)
            pool["cnt"][slot] += 16
            m.sem = sem
            m.val = pool["cnt"][slot]
            pool["last"][slot] = m
            self.q[eng].append(("i", fn, m, True))
        else:
            self.q[eng].append(("i", fn, m, False))
        self.lastm[eng] = m
        self.marks.append(m)
        for w in writes:
            self.wr[w] = m
            self.rd[w] = []
        for r in reads:
            self.rd.setdefault(r, []).append(m)
        return m

    def mm_group(self, fns_reads, writes=()):
        all_reads = []
        for _, rs in fns_reads:
            all_reads.extend(rs)
        for d in self._collect(all_reads, writes):
            self._dep_wait("tensor", d)
        m = Mark("tensor")
        for i, (fn, _) in enumerate(fns_reads):
            self.q["tensor"].append(
                ("i", fn, m if i == len(fns_reads) - 1 else None, False))
        self.lastm["tensor"] = m
        self.marks.append(m)
        for w in writes:
            self.wr[w] = m
            self.rd[w] = []
        for r in all_reads:
            self.rd.setdefault(r, []).append(m)
        return m

    def barrier(self):
        ms = {e: self.lastm[e] for e in ENGS if self.lastm[e] is not None}
        for e in ENGS:
            for pe, m in ms.items():
                if pe != e:
                    self._dep_wait(e, m)
        for k in list(self.wr):
            mk = self.wr[k]
            if mk.eng in ms:
                self.wr[k] = ms[mk.eng]
        self.rd = {k: [] for k in self.rd}
        return ms

    def dma_drain(self):
        for pool in self.pools.values():
            for m in pool["last"]:
                if m is not None:
                    for e in ENGS:
                        self._dep_wait(e, m)

    def finalize(self):
        needed = set()
        for e in ENGS:
            for it in self.q[e]:
                if it[0] == "w":
                    needed.add(id(it[1]))
        ecnt = {e: 0 for e in ENGS}
        self.final = {}
        for e in ENGS:
            out = []
            for it in self.q[e]:
                if it[0] == "i":
                    _, fn, m, is_dma = it
                    if m is None:
                        out.append(("i", fn, None, None, 0))
                    elif is_dma:
                        out.append(("i", fn, m, m.sem, 16))
                    elif id(m) in needed:
                        ecnt[e] += 1
                        m.val = ecnt[e]
                        m.sem = "_e_" + e
                        out.append(("i", fn, m, m.sem, 1))
                    else:
                        out.append(("i", fn, None, None, 0))
                else:
                    out.append(it)
            self.final[e] = out

    def run_engine(self, e, handle, sems):
        wm = {}
        for it in self.final[e]:
            if it[0] == "i":
                _, fn, m, semname, inc = it
                r = fn(handle)
                if m is not None:
                    r.then_inc(sems[semname], inc)
            else:
                m = it[1]
                if m.val is None:
                    continue
                if wm.get(m.sem, -1) >= m.val:
                    continue
                handle.wait_ge(sems[m.sem], m.val)
                wm[m.sem] = m.val


DMA_POOLS = {"d_x": 2, "d_w": 4, "d_c": 2, "d_sp": 8, "d_o": 2, "d_r": 6}


def _sem_names():
    names = ["_e_" + e for e in ENGS]
    for k, n in DMA_POOLS.items():
        names += ["%s%d" % (k, i) for i in range(n)]
    return names



def build_nc(cfg, dbg=(), stop=None):
    from contextlib import ExitStack
    LC, T, NB, TT = cfg.LC, cfg.T, cfg.NB, cfg.TT
    NTB = LC // 256
    nc = bass.Bass("TRN2", target_bir_lowering=False, debug=False)

    def din(name, shape, dt=F32):
        return nc.dram_tensor(name, list(shape), dt, kind="ExternalInput").ap()

    x_in = din("x_in", (LC, DM))
    xh3 = din("xh3", (3, DM))
    w_in = din("w_in", (DM, 2 * DI))
    w_out = din("w_out", (DI, DM))
    w_xp = din("w_xp", (DI, 96))
    w_dt = din("w_dt", (DTR, DI))
    w_g = din("w_g", (3072, DM))
    w_u = din("w_u", (3072, DM))
    w_d = din("w_d", (DM, 3072))
    chvec = din("chvec", (128, 7 * CD))
    alog = din("alog", (128, CD * DS))
    normw = din("normw", (128, DM))
    chain_ab = din("chain_ab", (128, 16))
    ident = din("ident", (128, 128))
    ones_r = din("ones_r", (1, 128))

    out = nc.dram_tensor("out", [LC, DM], F32, kind="ExternalOutput").ap()
    dbgo = {}
    for name, shape in dbg:
        dbgo[name] = nc.dram_tensor(name, list(shape), F32,
                                    kind="ExternalOutput").ap()

    def hbm(name, shape, dt):
        kind = "ExternalOutput" if (DBG_HBM and name in DBG_HBM_SET) \
            else "Internal"
        return nc.dram_tensor(name, list(shape), dt, kind=kind).ap()

    h_win = hbm("h_win", (16, 8, 128, 256), BF16)
    h_wout = hbm("h_wout", (4, 16, 128, 256), BF16)
    h_wu = hbm("h_wu", (3072, DM), BF16)
    h_wg = hbm("h_wg", (3072, DM), BF16)
    h_wd = hbm("h_wd", (DM, 3072), BF16)
    h_x1n = hbm("h_x1n", (LC, DM), F32)
    h_up = hbm("h_up", (3072, LC), F32)
    h_gate = hbm("h_gate", (3072, LC), F32)
    h_g2 = hbm("h_g2", (3072, LC), F32)
    h_ysave = hbm("h_ysave", (DI, T), F32)
    h_srsave = hbm("h_srsave", (DI, T), BF16)
    h_dsave = hbm("h_dsave", (DI, T), F32)
    h_dbg_u = hbm("h_dbg_u", (DI, T), F32)
    h_dbg_xi = hbm("h_dbg_xi", (DI, T), F32)
    h_dbg_xdbl = hbm("h_dbg_xdbl", (96, T), F32)
    h_dbg_xq = hbm("h_dbg_xq", (128, CM * LC), BF16)
    cc_in = hbm("cc_in", (128, 2 * CD * DS), F32)
    cc_out = hbm("cc_out", (cfg.NC * 128, 2 * CD * DS), F32)
    ccm_in = hbm("ccm_in", (1, 8), F32)
    ccm_out = hbm("ccm_out", (1, 8), F32)

    P = Prog(DMA_POOLS)

    def O(eng, fn, reads=(), writes=(), dma=None):
        return P.op(eng, fn, reads=reads, writes=writes, dma=dma)

    stack = ExitStack()
    sems = {}

    def _end():
        P.finalize()
        with nc.Block() as block:
            @block.sync
            def _(e):
                P.run_engine("sync", e, sems)

            @block.vector
            def _(e):
                P.run_engine("vector", e, sems)

            @block.scalar
            def _(e):
                P.run_engine("scalar", e, sems)

            @block.tensor
            def _(e):
                P.run_engine("tensor", e, sems)

            @block.gpsimd
            def _(e):
                P.run_engine("gpsimd", e, sems)

    with stack:
        for snm in _sem_names():
            sems[snm] = stack.enter_context(nc.semaphore("s_" + snm))

        per = ExitStack()

        def sb(st, name, shape, dt):
            return st.enter_context(nc.sbuf_tensor(name, list(shape), dt))

        s_chvec = sb(per, "s_chvec", (128, 7 * CD), F32)
        s_acols = sb(per, "s_acols", (128, CD * DS), F32)
        s_normw = sb(per, "s_normw", (128, DM), F32)
        s_chain = sb(per, "s_chain", (128, 16), F32)
        s_id32 = sb(per, "s_id32", (128, 128), F32)
        s_idbf = sb(per, "s_idbf", (128, 128), BF16)
        s_ones32 = sb(per, "s_ones32", (1, 128), F32)
        s_onesbf = sb(per, "s_onesbf", (1, 128), BF16)
        s_selbf = sb(per, "s_selbf", (32, 32 * 128), BF16)
        s_wxp = sb(per, "s_wxp", (128, CD * 96), BF16)
        s_wdt = sb(per, "s_wdt", (64, DI), BF16)
        s_scol = sb(per, "s_scol", (128, CD * DS), F32)
        s_dssum = sb(per, "s_dssum", (128, CD), F32)
        s_xmax = sb(per, "s_xmax", (128, 1), F32)
        s_stat = sb(per, "s_stat", (128, 16), F32)
        s_bc0 = sb(per, "s_bc0", (32, T), BF16)
        s_onet = sb(per, "s_onet", (128, T), F32)
        s_scl = sb(per, "s_scl", (128, 24), F32)
        s_sclr = sb(per, "s_sclr", (4, 16), F32)
        s_m1 = sb(per, "s_m1", (128, 4), F32)
        s_kc = sb(per, "s_kc", (128, 4), F32)
        nc.const_aps.register(F32, EPS, s_kc[:, 0:1])
        nc.const_aps.register(F32, 32768.5, s_kc[:, 1:2])
        nc.const_aps.register(F32, 0.5, s_kc[:, 2:3])
        nc.const_aps.register(F32, 8.5, s_kc[:, 3:4])

        p_mm = [stack.enter_context(nc.psum_tensor("p_mm%d" % i, [128, 512], F32))
                for i in range(4)]
        p_tr = [stack.enter_context(nc.psum_tensor("p_tr%d" % i, [128, 256], BF16))
                for i in range(2)]
        p_tf = [stack.enter_context(nc.psum_tensor("p_tf%d" % i, [128, 128], F32))
                for i in range(2)]

        pro = ExitStack()
        WSL = 3
        s_wload = sb(pro, "s_wload", (128, WSL * 4096), F32)
        s_wbf = sb(pro, "s_wbf", (128, WSL * 4096), BF16)
        s_wi32 = sb(pro, "s_wi32", (128, WSL * 3072), I32)
        s_qs = sb(pro, "s_qs", (128, WSL * 96), F32)
        s_qsi = sb(pro, "s_qsi", (128, WSL * 96), F32)
        s_alog = sb(pro, "s_alog", (128, CD * DS), F32)

        O("sync", lambda e: e.dma_start(out=s_chvec[:], in_=chvec),
          writes=["chvec"], dma="d_c")
        O("sync", lambda e: e.dma_start(out=s_alog[:], in_=alog),
          writes=["alog"], dma="d_c")
        O("sync", lambda e: e.dma_start(out=s_normw[:], in_=normw),
          writes=["normw"], dma="d_c")
        O("sync", lambda e: e.dma_start(out=s_chain[:], in_=chain_ab),
          writes=["chain"], dma="d_c")
        O("sync", lambda e: e.dma_start(out=s_id32[:], in_=ident),
          writes=["id32"], dma="d_c")
        O("sync", lambda e: e.dma_start(out=s_ones32[:], in_=ones_r),
          writes=["ones32"], dma="d_c")
        O("scalar", lambda e: e.copy(
            s_selbf[:].rearrange("p (n m) -> p n m", m=128),
            s_id32[0:32, 0:32].rearrange(
                "p n -> p n ()").broadcast_to([32, 32, 128])),
          reads=["id32"], writes=["selbf"])
        O("scalar", lambda e: e.activation(s_acols[:], s_alog[:], AT.Exp),
          reads=["alog"], writes=["acols"])
        O("vector", lambda e: e.tensor_scalar_mul(s_acols[:], s_acols[:], -1.0),
          reads=["acols"], writes=["acols"])
        O("scalar", lambda e: e.copy(s_idbf[:], s_id32[:]),
          reads=["id32"], writes=["idbf"])
        O("scalar", lambda e: e.copy(s_onesbf[:], s_ones32[:]),
          reads=["ones32"], writes=["onesbf"])
        O("vector", lambda e: e.memset(s_onet[:], 1.0), writes=["onet"])
        O("vector", lambda e: e.memset(s_scol[:], 0.0), writes=["scolA"])
        O("vector", lambda e: e.memset(s_dssum[:], 0.0), writes=["dssum"])
        O("vector", lambda e: e.memset(s_xmax[:], 0.0), writes=["xmax"])
        O("vector", lambda e: e.memset(s_sclr[:], 0.0), writes=["sclr"])
        O("vector", lambda e: e.memset(s_kc[:, 0:1], EPS), writes=["kconst"])
        O("vector", lambda e: e.memset(s_kc[:, 1:2], 32768.5),
          reads=["kconst"], writes=["kconst"])
        O("vector", lambda e: e.memset(s_kc[:, 2:3], 0.5),
          reads=["kconst"], writes=["kconst"])
        O("vector", lambda e: e.memset(s_kc[:, 3:4], 8.5),
          reads=["kconst"], writes=["kconst"])
        O("scalar", lambda e: e.dma_start(out=ccm_in[:, :], in_=s_sclr[0:1, 0:8]),
          reads=["sclr"], writes=["ccmin_h"], dma="d_sp")

        for k in range(8):
            sl = k % WSL
            key = "wld%d" % sl
            wls = s_wload[:, 4096 * sl:4096 * (sl + 1)]
            wbs = s_wbf[:, 4096 * sl:4096 * (sl + 1)]
            for h, pool in ((0, "d_w"), (1, "d_r")):
                O("sync", lambda e, k=k, wls=wls, h=h: e.dma_start(
                    out=wls[:, 2048 * h:2048 * (h + 1)],
                    in_=w_in[128 * k:128 * (k + 1),
                             2048 * h:2048 * (h + 1)]),
                  reads=[key + "h%d" % h], writes=[key + "h%d" % h], dma=pool)
            O("gpsimd", lambda e, wls=wls, wbs=wbs: e.tensor_copy(wbs, wls),
              reads=[key + "h0", key + "h1", "wbf%d" % sl],
              writes=["wbf%d" % sl])
            for mg in range(16):
                O("scalar", lambda e, k=k, mg=mg, wbs=wbs: e.dma_start(
                    out=h_win[mg, k], in_=wbs[:, 256 * mg:256 * (mg + 1)]),
                  reads=["wbf%d" % sl], writes=["h_win"], dma="d_sp")
        for k in range(16):
            sl = k % WSL
            key = "wld%d" % sl
            wls = s_wload[:, 4096 * sl:4096 * sl + DM]
            wbs = s_wbf[:, 4096 * sl:4096 * sl + DM]
            for h, pool in ((0, "d_w"), (1, "d_r")):
                O("sync", lambda e, k=k, wls=wls, h=h: e.dma_start(
                    out=wls[:, 512 * h:512 * (h + 1)],
                    in_=w_out[128 * k:128 * (k + 1),
                              512 * h:512 * (h + 1)]),
                  reads=[key + "h%d" % h], writes=[key + "h%d" % h], dma=pool)
            O("gpsimd", lambda e, wls=wls, wbs=wbs: e.tensor_copy(wbs, wls),
              reads=[key + "h0", key + "h1", "wbf%d" % sl],
              writes=["wbf%d" % sl])
            for mg in range(4):
                O("scalar", lambda e, k=k, mg=mg, wbs=wbs: e.dma_start(
                    out=h_wout[mg, k], in_=wbs[:, 256 * mg:256 * (mg + 1)]),
                  reads=["wbf%d" % sl], writes=["h_wout"], dma="d_sp")
        for k in range(CD):
            sl = k % WSL
            key = "wld%d" % sl
            wls = s_wload[:, 4096 * sl:4096 * sl + 96]
            O("sync", lambda e, k=k, wls=wls: e.dma_start(
                out=wls, in_=w_xp[128 * k:128 * (k + 1), :]),
              reads=[key + "h0"], writes=[key + "h0"], dma="d_w")
            O("gpsimd", lambda e, k=k, wls=wls: e.tensor_copy(
                s_wxp[:, 96 * k:96 * (k + 1)], wls),
              reads=[key + "h0"], writes=["wxp"])
        O("sync", lambda e: e.dma_start(out=s_wload[0:64, 0:DI], in_=w_dt),
          reads=["wld0h0", "wld0h1"], writes=["wld0h0", "wld0h1"], dma="d_w")
        O("scalar", lambda e: e.copy(s_wdt[:], s_wload[0:64, 0:DI]),
          reads=["wld0h0", "wld0h1"], writes=["wdt"])

        def ffn_prep(w_ap, h_dst, hkey, nrows, ncols):
            nb32 = ncols // 32
            for k in range(nrows // 128):
                sl = k % WSL
                key = "wld%d" % sl
                kq, kqi, ki32, kbf = ("qs%d" % sl, "qsi%d" % sl,
                                      "wi32%d" % sl, "wbf%d" % sl)
                wls = s_wload[:, 4096 * sl:4096 * sl + ncols]
                wbs = s_wbf[:, 4096 * sl:4096 * sl + ncols]
                wis = s_wi32[:, 3072 * sl:3072 * sl + ncols]
                qss = s_qs[:, 96 * sl:96 * sl + nb32]
                qis = s_qsi[:, 96 * sl:96 * sl + nb32]
                hc = ncols // 2
                for h, pool in ((0, "d_w"), (1, "d_r")):
                    O("sync", lambda e, k=k, wls=wls, h=h, hc=hc: e.dma_start(
                        out=wls[:, hc * h:hc * (h + 1)],
                        in_=w_ap[128 * k:128 * (k + 1), hc * h:hc * (h + 1)]),
                      reads=[key + "h%d" % h], writes=[key + "h%d" % h],
                      dma=pool)
                rv = wls.rearrange("p (b e) -> p b e", e=32)
                O("vector", lambda e, rv=rv, qss=qss: e.tensor_reduce(
                    qss, rv, axis=AX, op=OP.max,
                    apply_absolute_value=True),
                  reads=[key + "h0", key + "h1", kq], writes=[kq])
                O("vector", lambda e, qss=qss: e.tensor_scalar(
                    qss, qss,
                    float(np.float32(1.0) / np.float32(7.0)), 1e-12,
                    OP.mult, OP.max),
                  reads=[kq], writes=[kq])
                O("vector", lambda e, qss=qss, qis=qis: e.reciprocal(qis, qss),
                  reads=[kq, kqi], writes=[kqi])

                def bcq(t):
                    return t.rearrange("p b -> p b ()",
                                       b=nb32).broadcast_to([128, nb32, 32])
                O("vector", lambda e, rv=rv, qis=qis: e.tensor_tensor(
                    rv, rv, bcq(qis), OP.mult),
                  reads=[key + "h0", key + "h1", kqi],
                  writes=[key + "h0", key + "h1"])
                O("scalar", lambda e, wis=wis, wls=wls: e.activation(
                    wis, wls, AT.Identity),
                  reads=[key + "h0", key + "h1", ki32], writes=[ki32])
                O("vector", lambda e, wbs=wbs, wis=wis, qss=qss:
                  e.tensor_tensor(
                    wbs.rearrange("p (b e) -> p b e", e=32),
                    wis.rearrange("p (b e) -> p b e", e=32),
                    bcq(qss), OP.mult),
                  reads=[ki32, kq, kbf], writes=[kbf])
                O("scalar", lambda e, k=k, wbs=wbs: e.dma_start(
                    out=h_dst[128 * k:128 * (k + 1), :], in_=wbs),
                  reads=[kbf], writes=[hkey], dma="d_sp")

        ffn_prep(w_u, h_wu, "h_wu", 3072, DM)
        ffn_prep(w_g, h_wg, "h_wg", 3072, DM)
        ffn_prep(w_d, h_wd, "h_wd", DM, 3072)
        pro.close()
        P.barrier()
        if stop == "pro":
            _end()
            return nc

        mam = ExitStack()
        s_win = sb(mam, "s_win", (128, 2 * 2048), BF16)
        s_wo = sb(mam, "s_wo", (128, 2 * 4096), BF16)
        s_x = sb(mam, "s_x", (128, 2 * TT * DM), F32)
        s_xnbf = sb(mam, "s_xnbf", (128, TT * DM), BF16)
        s_xnfm = sb(mam, "s_xnfm", (128, CM * T), BF16)
        s_xnfm3 = sb(mam, "s_xnfm3", (128, 3 * CM), BF16)
        s_xiw = sb(mam, "s_xiw", (128, 2 * 262), F32)
        s_halo = sb(mam, "s_halo", (128, CD * 3), F32)
        s_u = sb(mam, "s_u", (128, CD * T), F32)
        s_ubf = sb(mam, "s_ubf", (128, CD * T), BF16)
        s_sres = sb(mam, "s_sres", (128, CD * T), BF16)
        s_drbf = sb(mam, "s_drbf", (64, T), BF16)
        s_bcrow = sb(mam, "s_bcrow", (32, T), BF16)
        s_delta = sb(mam, "s_delta", (128, CD * T), F32)
        s_brep = sb(mam, "s_brep", (128, DS * T), BF16)
        s_crep = sb(mam, "s_crep", (128, DS * T), BF16)
        s_sp = sb(mam, "s_sp", (128, DS * (T + 1)), F32)
        s_dAq = sb(mam, "s_dAq", (128, 2 * 4 * (T + 1)), F32)
        s_wdu = sb(mam, "s_wdu", (128, T), F32)
        s_ybuf = sb(mam, "s_ybuf", (128, T), F32)
        s_ygbf = sb(mam, "s_ygbf", (128, 2 * CD * T), BF16)
        s_attn = sb(mam, "s_attn", (128, CM * T), F32)
        s_x1 = sb(mam, "s_x1", (128, TT * DM), F32)
        s_e1 = sb(mam, "s_e1", (128, 2 * T), F32)
        s_xb = sb(mam, "s_xb", (128, 2 * T), F32)
        s_gath = sb(mam, "s_gath", (128, 2 * CD * DS), F32)
        s_sin = sb(mam, "s_sin", (128, CD * DS), F32)
        s_tmp = sb(mam, "s_tmp", (128, CD * DS), F32)
        s_cums = sb(mam, "s_cums", (128, T), F32)

        import sys
        print("SBUF remaining after mamba allocs:", nc.sbuf_bytes_remaining,
              file=sys.stderr)
        O("vector", lambda e: e.memset(s_dAq[:], 0.0),
          writes=["dAq0", "dAq1"])
        O("sync", lambda e: e.dma_start(out=s_x[0:3, 0:DM], in_=xh3),
          writes=["xst0"], dma="d_x")
        O("scalar", lambda e: e.activation(
            s_x1[0:3, 0:DM], s_x[0:3, 0:DM], AT.Square,
            accum_out=s_stat[0:3, 0:1]),
          reads=["xst0", "x1t0"], writes=["hst", "x1t0"])
        O("scalar", lambda e: e.activation(
            s_stat[0:3, 1:2], s_stat[0:3, 0:1], AT.Sqrt,
            scale=1.0 / DM, bias=EPS),
          reads=["hst"], writes=["hst1"])
        O("vector", lambda e: e.reciprocal(s_stat[0:3, 2:3], s_stat[0:3, 1:2]),
          reads=["hst1"], writes=["hst2"])
        O("vector", lambda e: e.scalar_tensor_tensor(
            s_xnbf[0:3, 0:DM], s_x[0:3, 0:DM], s_stat[0:3, 2:3],
            s_normw[0:3, :], OP.mult, OP.mult),
          reads=["xst0", "hst2", "normw"], writes=["hxn"])
        for kc in range(CM):
            pk = "ptr%d" % (kc % 2)
            O("tensor", lambda e, kc=kc: e.transpose(
                p_tr[kc % 2][:, 0:3], s_xnbf[0:3, 128 * kc:128 * (kc + 1)],
                s_idbf[0:3, 0:3]),
              reads=["hxn", "idbf"], writes=[pk])
            O("scalar", lambda e, kc=kc, pk=pk: e.copy(
                s_xnfm3[:, 3 * kc:3 * kc + 3], p_tr[kc % 2][:, 0:3]),
              reads=[pk], writes=["hxnf%d" % kc])
        for mg in range(8):
            O("sync", lambda e, mg=mg: e.dma_start(
                out=s_win[:, 0:2048].rearrange("p (k j) -> p k j", j=256),
                in_=h_win[mg].rearrange("k p j -> p k j")),
              reads=["h_win"], writes=["win0h0", "win0h1"], dma="d_w")
            for sub in range(2):
                c = 2 * mg + sub
                fr = []
                for k in range(8):
                    fr.append((
                        (lambda e, k=k, sub=sub: e.matmul(
                            p_mm[2][:, 0:3],
                            s_win[:, 256 * k + 128 * sub:
                                  256 * k + 128 * sub + 128],
                            s_xnfm3[:, 3 * k:3 * k + 3],
                            start=(k == 0), stop=(k == 7))),
                        ["win0h0", "win0h1", "hxnf%d" % k]))
                P.mm_group(fr, writes=["pmm2"])
                O("scalar", lambda e, c=c: e.copy(
                    s_halo[:, 3 * c:3 * c + 3], p_mm[2][:, 0:3]),
                  reads=["pmm2"], writes=["halo%d" % c])

        def head_pre(b):
            xslot = b % 2
            t0 = b * T
            for i in range(TT):
                O("sync", lambda e, i=i, t0=t0, xslot=xslot: e.dma_start(
                    out=s_x[:, 2048 * xslot + DM * i:2048 * xslot + DM * (i + 1)],
                    in_=x_in[t0 + 128 * i:t0 + 128 * (i + 1), :]),
                  writes=["xst%d" % xslot], dma="d_x")
            for i in range(TT):
                xt = s_x[:, 2048 * xslot + DM * i:2048 * xslot + DM * (i + 1)]
                O("scalar", lambda e, xt=xt: e.activation(
                    s_x1[:, 0:DM], xt, AT.Square, accum_out=s_stat[:, 0:1]),
                  reads=["xst%d" % xslot, "x1t0"], writes=["st0", "x1t0"])
                O("scalar", lambda e: e.activation(
                    s_stat[:, 1:2], s_stat[:, 0:1], AT.Sqrt,
                    scale=1.0 / DM, bias=EPS),
                  reads=["st0", "kconst"], writes=["st1"])
                O("vector", lambda e: e.reciprocal(
                    s_stat[:, 2:3], s_stat[:, 1:2]),
                  reads=["st1"], writes=["st2"])
                O("vector", lambda e, xt=xt, i=i: e.scalar_tensor_tensor(
                    s_xnbf[:, DM * i:DM * (i + 1)], xt, s_stat[:, 2:3],
                    s_normw[:], OP.mult, OP.mult),
                  reads=["xst%d" % xslot, "st2", "normw"],
                  writes=["xnbf%d" % i])
                for kc in range(CM):
                    pk = "ptr%d" % (kc % 2)
                    O("tensor", lambda e, i=i, kc=kc: e.transpose(
                        p_tr[kc % 2][:, 0:128],
                        s_xnbf[:, DM * i + 128 * kc:DM * i + 128 * (kc + 1)],
                        s_idbf[:]),
                      reads=["xnbf%d" % i, "idbf"], writes=[pk])
                    O("scalar", lambda e, i=i, kc=kc, pk=pk: e.copy(
                        s_xnfm[:, T * kc + 128 * i:T * kc + 128 * (i + 1)],
                        p_tr[kc % 2][:, 0:128]),
                      reads=[pk], writes=["xnfm%d" % kc])
        def head_mg_dma(b, mg, slot):
            ws = "win%d" % slot
            for h, pool in ((0, "d_w"), (1, "d_r")):
                O("sync", lambda e, mg=mg, slot=slot, h=h: e.dma_start(
                    out=s_win[:, slot * 2048 + 1024 * h:
                              slot * 2048 + 1024 * (h + 1)]
                    .rearrange("p (k j) -> p k j", j=256),
                    in_=h_win[mg, 4 * h:4 * (h + 1)].rearrange(
                        "k p j -> p k j")),
                  reads=["h_win"], writes=[ws + "h%d" % h], dma=pool)

        def head_mg(b, mg, slot=None, dma=True):
            if True:
                if slot is None:
                    slot = mg % 2
                ws = "win%d" % slot
                if dma:
                    head_mg_dma(b, mg, slot)
                for sub in range(2):
                    c = 2 * mg + sub
                    pm = "pmm%d" % (c % 2)
                    fr = []
                    for k in range(8):
                        fr.append((
                            (lambda e, k=k, sub=sub, mg=mg, c=c, slot=slot:
                             e.matmul(
                                p_mm[c % 2][:, 0:T],
                                s_win[:, slot * 2048 + 256 * (k % 4)
                                      + 1024 * (k // 4) + 128 * sub:
                                      slot * 2048 + 256 * (k % 4)
                                      + 1024 * (k // 4) + 128 * sub + 128],
                                s_xnfm[:, T * k:T * (k + 1)],
                                start=(k == 0), stop=(k == 7))),
                            [ws + "h%d" % (k // 4), "xnfm%d" % k]))
                    P.mm_group(fr, writes=[pm])
                    if c < CD:
                        xw = "xiw%d" % (c % 2)
                        base = (c % 2) * 262
                        O("scalar", lambda e, c=c, base=base: e.copy(
                            s_xiw[:, base + 3:base + 3 + T],
                            p_mm[c % 2][:, 0:T]),
                          reads=[pm], writes=[xw])
                        O("scalar", lambda e, c=c, base=base: e.copy(
                            s_xiw[:, base:base + 3], s_halo[:, 3 * c:3 * c + 3]),
                          reads=["halo%d" % c], writes=[xw + "h"])
                        O("vector", lambda e, c=c, base=base: e.tensor_scalar(
                            s_u[:, T * c:T * (c + 1)],
                            s_xiw[:, base:base + T],
                            s_chvec[:, 7 * c:7 * c + 1], None, OP.mult),
                          reads=[xw, xw + "h", "chvec"], writes=["uc%d" % c])
                        for j in range(1, 4):
                            O("vector", lambda e, c=c, base=base, j=j:
                              e.scalar_tensor_tensor(
                                  s_u[:, T * c:T * (c + 1)],
                                  s_xiw[:, base + j:base + j + T],
                                  s_chvec[:, 7 * c + j:7 * c + j + 1],
                                  s_u[:, T * c:T * (c + 1)],
                                  OP.mult, OP.add),
                              reads=[xw, xw + "h", "chvec", "uc%d" % c],
                              writes=["uc%d" % c])
                        O("gpsimd", lambda e, c=c, base=base: e.tensor_copy(
                            s_halo[:, 3 * c:3 * c + 3],
                            s_xiw[:, base + T:base + T + 3]),
                          reads=[xw], writes=["halo%d" % c])
                        O("scalar", lambda e, c=c: e.activation(
                            s_u[:, T * c:T * (c + 1)],
                            s_u[:, T * c:T * (c + 1)], AT.Silu,
                            bias=s_chvec[:, 7 * c + 4:7 * c + 5]),
                          reads=["uc%d" % c, "chvec"], writes=["uc%d" % c])
                        O("gpsimd", lambda e, c=c: e.tensor_copy(
                            s_ubf[:, T * c:T * (c + 1)],
                            s_u[:, T * c:T * (c + 1)]),
                          reads=["uc%d" % c], writes=["ubf%d" % c])
                        if XPROJ_IN_MG:
                            O("tensor", lambda e, c=c: e.matmul(
                                p_mm[2][0:96, 0:T],
                                s_wxp[:, 96 * c:96 * (c + 1)],
                                s_ubf[:, T * c:T * (c + 1)],
                                start=(c == 0), stop=(c == CD - 1)),
                              reads=["wxp", "ubf%d" % c], writes=["pmm2"])
                        if b == 0:
                            O("scalar", lambda e, c=c: e.dma_start(
                                out=h_dbg_u[128 * c:128 * (c + 1), :],
                                in_=s_u[:, T * c:T * (c + 1)]),
                              reads=["uc%d" % c], dma="d_sp")
                            O("scalar", lambda e, c=c, base=base: e.dma_start(
                                out=h_dbg_xi[128 * c:128 * (c + 1), :],
                                in_=s_xiw[:, base + 3:base + 3 + T]),
                              reads=[xw], dma="d_sp")
                    else:
                        c2 = c - CD
                        O("scalar", lambda e, c2=c2, c=c: e.activation(
                            s_sres[:, T * c2:T * (c2 + 1)],
                            p_mm[c % 2][:, 0:T], AT.Silu),
                          reads=[pm], writes=["sres%d" % c2])
        def head_post(b):
            if not XPROJ_IN_MG:
                fr = []
                for k in range(CD):
                    fr.append((
                        (lambda e, k=k: e.matmul(
                            p_mm[2][0:96, 0:T], s_wxp[:, 96 * k:96 * (k + 1)],
                            s_ubf[:, T * k:T * (k + 1)],
                            start=(k == 0), stop=(k == CD - 1))),
                        ["wxp", "ubf%d" % k]))
                P.mm_group(fr, writes=["pmm2"])
            if b == 0:
                O("scalar", lambda e: e.copy(s_xb[0:96, 0:T], p_mm[2][0:96, 0:T]),
                  reads=["pmm2", "xb0"], writes=["xb0"])
                O("scalar", lambda e: e.dma_start(
                    out=h_dbg_xdbl[:, :], in_=s_xb[0:96, 0:T]),
                  reads=["xb0"], dma="d_sp")
            O("scalar", lambda e: e.copy(s_drbf[:, 0:T], p_mm[2][0:64, 0:T]),
              reads=["pmm2"], writes=["drbf"])
            O("scalar", lambda e: e.copy(s_bcrow[:, 0:T], p_mm[2][64:96, 0:T]),
              reads=["pmm2"], writes=["bcrow"])
            if b == 0:
                O("scalar", lambda e: e.copy(s_bc0[:, 0:T], s_bcrow[:, 0:T]),
                  reads=["bcrow"], writes=["bc0"])
            for n in range(DS):
                pb = "pmm%d" % (2 + (n % 2))
                O("tensor", lambda e, n=n: e.matmul(
                    p_mm[2 + (n % 2)][:, 0:T],
                    s_selbf[:, 128 * n:128 * (n + 1)],
                    s_bcrow[0:32, 0:T], start=True, stop=True),
                  reads=["bcrow", "selbf"], writes=[pb])
                O("scalar", lambda e, n=n, pb=pb: e.copy(
                    s_brep[:, T * n:T * (n + 1)], p_mm[2 + (n % 2)][:, 0:T]),
                  reads=[pb, "brepA"], writes=["brepA"])
                O("tensor", lambda e, n=n: e.matmul(
                    p_mm[2 + (n % 2)][:, 0:T],
                    s_selbf[:, 128 * (16 + n):128 * (17 + n)],
                    s_bcrow[0:32, 0:T], start=True, stop=True),
                  reads=["bcrow", "selbf"], writes=[pb])
                O("scalar", lambda e, n=n, pb=pb: e.copy(
                    s_crep[:, T * n:T * (n + 1)], p_mm[2 + (n % 2)][:, 0:T]),
                  reads=[pb, "crepA"], writes=["crepA"])
            def dt_finish(c):
                xo = (c % 2) * T
                O("vector", lambda e, c=c, xo=xo: e.scalar_tensor_tensor(
                    s_delta[:, T * c:T * (c + 1)], s_xb[:, xo:xo + T], 0.0,
                    s_e1[:, xo:xo + T], OP.max, OP.add),
                  reads=["xb%d" % (c % 2), "e1%d" % (c % 2)],
                  writes=["dl%d" % c])
                O("vector", lambda e, c=c: e.reduce_sum(
                    s_stat[:, 8:9], s_delta[:, T * c:T * (c + 1)], axis=AX),
                  reads=["dl%d" % c], writes=["dred"])
                O("vector", lambda e, c=c: e.tensor_add(
                    s_dssum[:, c:c + 1], s_dssum[:, c:c + 1], s_stat[:, 8:9]),
                  reads=["dred", "dssum"], writes=["dssum"])
            for c in range(CD):
                pm = "pmm%d" % (c % 2)
                xo = (c % 2) * T
                O("tensor", lambda e, c=c: e.matmul(
                    p_mm[c % 2][:, 0:T], s_wdt[:, 128 * c:128 * (c + 1)],
                    s_drbf[:, 0:T], start=True, stop=True),
                  reads=["wdt", "drbf"], writes=[pm])
                O("scalar", lambda e, c=c, pm=pm, xo=xo: e.activation(
                    s_xb[:, xo:xo + T], p_mm[c % 2][:, 0:T], AT.Identity,
                    bias=s_chvec[:, 7 * c + 6:7 * c + 7]),
                  reads=[pm, "chvec"], writes=["xb%d" % (c % 2)])
                O("vector", lambda e, xo=xo, c=c: e.scalar_tensor_tensor(
                    s_e1[:, xo:xo + T], s_xb[:, xo:xo + T], -1.0,
                    s_xb[:, xo:xo + T], OP.mult, OP.min),
                  reads=["xb%d" % (c % 2)], writes=["e1%d" % (c % 2)])
                O("scalar", lambda e, xo=xo, c=c: e.activation(
                    s_e1[:, xo:xo + T], s_e1[:, xo:xo + T], AT.Exp),
                  reads=["e1%d" % (c % 2)], writes=["e1%d" % (c % 2)])
                O("scalar", lambda e, xo=xo, c=c: e.activation(
                    s_e1[:, xo:xo + T], s_e1[:, xo:xo + T], AT.Ln, bias=1.0),
                  reads=["e1%d" % (c % 2)], writes=["e1%d" % (c % 2)])
                if c >= 1:
                    dt_finish(c - 1)
            dt_finish(CD - 1)

        def block_scan_chunk(b, c):
            TP = T + 1
            O("vector", lambda e, c=c: e.tensor_mul(
                s_wdu[:, 0:T], s_delta[:, T * c:T * (c + 1)],
                s_u[:, T * c:T * (c + 1)]),
              reads=["dl%d" % c, "uc%d" % c], writes=["wdu"])

            spv = s_sp[:].rearrange("p (n t) -> p n t", n=DS)
            O("vector", lambda e, spv=spv: e.tensor_tensor(
                spv[:, :, 1:],
                s_wdu[:, 0:T].rearrange("p t -> p () t").broadcast_to(
                    [128, DS, T]),
                s_brep[:].rearrange("p (n t) -> p n t", n=DS), OP.mult),
              reads=["wdu", "brepA", "spall"], writes=["spall"])
            O("vector", lambda e, c=c, spv=spv: e.tensor_copy(
                spv[:, 0:8, 0:1], s_scol[:, DS * c:DS * c + 8].rearrange(
                    "p n -> p n ()")),
              reads=["scolA", "spall"], writes=["spall"])
            O("gpsimd", lambda e, c=c, spv=spv: e.tensor_copy(
                spv[:, 8:16, 0:1],
                s_scol[:, DS * c + 8:DS * c + 16].rearrange("p n -> p n ()")),
              reads=["scolA", "spall"], writes=["spall"])
            QP = 4 * (T + 1)
            for g in range(4):
                dk = "dAq%d" % (g % 2)
                for n in range(4 * g, 4 * g + 4):
                    O("scalar", lambda e, c=c, n=n, g=g: e.activation(
                        s_dAq[:, (g % 2) * QP + TP * (n - 4 * g) + 1:
                              (g % 2) * QP + TP * (n - 4 * g + 1)],
                        s_delta[:, T * c:T * (c + 1)], AT.Exp,
                        scale=s_acols[:, DS * c + n:DS * c + n + 1]),
                      reads=["dl%d" % c, "acols", dk], writes=[dk])
                O("vector", lambda e, g=g: e.tensor_tensor_scan(
                    s_sp[:, QP * g:QP * (g + 1)],
                    s_dAq[:, (g % 2) * QP:(g % 2 + 1) * QP],
                    s_sp[:, QP * g:QP * (g + 1)], 0.0,
                    OP.mult, OP.add),
                  reads=[dk, "spall"], writes=["spall"])
            O("gpsimd", lambda e, c=c: e.tensor_copy(
                s_scol[:, DS * c:DS * (c + 1)],
                s_sp[:].rearrange("p (n t) -> p n t", n=DS)[:, :, T:T + 1]),
              reads=["spall", "scolA"], writes=["scolA"])
            O("vector", lambda e, spv=spv: e.tensor_tensor(
                spv[:, :, 1:], spv[:, :, 1:],
                s_crep[:].rearrange("p (n t) -> p n t", n=DS), OP.mult),
              reads=["spall", "crepA"], writes=["spall"])
            TP = T + 1
            for lo, eng in ((0, "vector"), (8, "gpsimd")):
                for step in (1, 2, 4):
                    O(eng, lambda e, step=step, lo=lo, spv=spv:
                      e.tensor_tensor(
                          spv[:, lo:lo + 8:2 * step, 1:],
                          spv[:, lo:lo + 8:2 * step, 1:],
                          spv[:, lo + step:lo + 8:2 * step, 1:],
                          OP.add),
                      reads=["spall"], writes=["spall"])
            O("vector", lambda e, spv=spv: e.tensor_tensor(
                spv[:, 0:1, 1:], spv[:, 0:1, 1:], spv[:, 8:9, 1:], OP.add),
              reads=["spall"], writes=["spall"])
            O("vector", lambda e, c=c: e.scalar_tensor_tensor(
                s_ybuf[:, 0:T], s_u[:, T * c:T * (c + 1)],
                s_chvec[:, 7 * c + 5:7 * c + 6], s_sp[:, 1:T + 1],
                OP.mult, OP.add),
              reads=["uc%d" % c, "chvec", "spall", "ybuf"], writes=["ybuf"])

        def block_gate_chunk(b, c):
            ys = (b % 2) * CD * T
            O("vector", lambda e, c=c, ys=ys: e.tensor_mul(
                s_ygbf[:, ys + T * c:ys + T * (c + 1)], s_ybuf[:, 0:T],
                s_sres[:, T * c:T * (c + 1)]),
              reads=["ybuf", "sres%d" % c], writes=["yg%d.%d" % (b % 2, c)])

        def tail_mm_dma(b, mg):
            ws = "wo%d" % (mg % 2)
            for h, pool in ((0, "d_w"), (1, "d_r")):
                O("sync", lambda e, mg=mg, h=h: e.dma_start(
                    out=s_wo[:, (mg % 2) * 4096 + 2048 * h:
                             (mg % 2) * 4096 + 2048 * (h + 1)]
                    .rearrange("p (k j) -> p k j", j=256),
                    in_=h_wout[mg, 8 * h:8 * (h + 1)].rearrange(
                        "k p j -> p k j")),
                  reads=["h_wout"], writes=[ws + "h%d" % h], dma=pool)

        def tail_mm(b, xslot, mg, dma=True):
            ys = (b % 2) * CD * T
            if True:
                ws = "wo%d" % (mg % 2)
                if dma:
                    tail_mm_dma(b, mg)
                for sub in range(2):
                    m = 2 * mg + sub
                    pm = "pmm%d" % (m % 2)
                    fr = []
                    for k in range(CD):
                        fr.append((
                            (lambda e, k=k, mg=mg, sub=sub, m=m, ys=ys:
                             e.matmul(
                                 p_mm[m % 2][:, 0:T],
                                 s_wo[:, (mg % 2) * 4096 + 256 * (k % 8)
                                       + 2048 * (k // 8) + 128 * sub:
                                       (mg % 2) * 4096 + 256 * (k % 8)
                                       + 2048 * (k // 8) + 128 * sub + 128],
                                 s_ygbf[:, ys + T * k:ys + T * (k + 1)],
                                 start=(k == 0), stop=(k == CD - 1))),
                            [ws + "h%d" % (k // 8), "yg%d.%d" % (b % 2, k)]))
                    P.mm_group(fr, writes=[pm])
                    O("scalar", lambda e, m=m, pm=pm: e.copy(
                        s_attn[:, T * m:T * (m + 1)], p_mm[m % 2][:, 0:T]),
                      reads=[pm, "lnsq"], writes=["attn%d" % m])
        def tail_fin(b, xslot):
            t0 = b * T
            for m in range(CM):
                for i in range(TT):
                    pf = "ptf%d" % ((m * TT + i) % 2)
                    O("tensor", lambda e, m=m, i=i: e.transpose(
                        p_tf[(m * TT + i) % 2][:],
                        s_attn[:, T * m + 128 * i:T * m + 128 * (i + 1)],
                        s_id32[:]),
                      reads=["attn%d" % m, "id32"], writes=[pf])
                    O("vector", lambda e, m=m, i=i, xslot=xslot, pf=pf:
                      e.tensor_add(
                          s_x1[:, DM * i + 128 * m:DM * i + 128 * (m + 1)],
                          p_tf[(m * TT + i) % 2][:],
                          s_x[:, 2048 * xslot + DM * i + 128 * m:
                              2048 * xslot + DM * i + 128 * (m + 1)]),
                      reads=[pf, "xst%d" % xslot], writes=["x1t%d" % i])
            for i in range(TT):
                x1t = s_x1[:, DM * i:DM * (i + 1)]
                O("scalar", lambda e, x1t=x1t: e.activation(
                    s_attn[:, 0:DM], x1t, AT.Square, accum_out=s_stat[:, 3:4]),
                  reads=["x1t%d" % i, "attn0", "attn1", "attn2", "attn3",
                         "attn4", "attn5", "attn6", "attn7"],
                  writes=["lst0", "lnsq"])
                O("vector", lambda e, x1t=x1t: e.reduce_sum(
                    s_stat[:, 4:5], x1t, axis=AX),
                  reads=["x1t%d" % i], writes=["lst1"])
                O("vector", lambda e: e.tensor_scalar_mul(
                    s_stat[:, 5:6], s_stat[:, 4:5], 1.0 / DM),
                  reads=["lst1"], writes=["lmean"])
                O("vector", lambda e: e.tensor_mul(
                    s_stat[:, 6:7], s_stat[:, 5:6], s_stat[:, 5:6]),
                  reads=["lmean"], writes=["lm2"])
                O("vector", lambda e: e.scalar_tensor_tensor(
                    s_stat[:, 7:8], s_stat[:, 3:4], 1.0 / DM,
                    s_stat[:, 6:7], OP.mult, OP.subtract),
                  reads=["lst0", "lm2"], writes=["lvar"])
                O("scalar", lambda e: e.activation(
                    s_stat[:, 9:10], s_stat[:, 7:8], AT.Sqrt, bias=EPS),
                  reads=["lvar", "kconst"], writes=["lsd"])
                O("vector", lambda e: e.reciprocal(
                    s_stat[:, 10:11], s_stat[:, 9:10]),
                  reads=["lsd"], writes=["lrs"])
                O("vector", lambda e: e.tensor_scalar_mul(
                    s_stat[:, 11:12], s_stat[:, 5:6], -1.0),
                  reads=["lmean"], writes=["lnm"])
                O("vector", lambda e, x1t=x1t, i=i: e.tensor_scalar(
                    x1t, x1t, s_stat[:, 11:12], s_stat[:, 10:11],
                    OP.add, OP.mult),
                  reads=["x1t%d" % i, "lnm", "lrs"], writes=["x1t%d" % i])
                O("vector", lambda e, x1t=x1t: e.tensor_reduce(
                    s_stat[:, 12:13], x1t, axis=AX, op=OP.max,
                    apply_absolute_value=True),
                  reads=["x1t%d" % i], writes=["lmx"])
                O("vector", lambda e: e.tensor_max(
                    s_xmax[:], s_xmax[:], s_stat[:, 12:13]),
                  reads=["lmx", "xmax"], writes=["xmax"])
                O("scalar", lambda e, i=i, t0=t0, x1t=x1t: e.dma_start(
                    out=h_x1n[t0 + 128 * i:t0 + 128 * (i + 1), :], in_=x1t),
                  reads=["x1t%d" % i], writes=["h_x1n"], dma="d_sp")

        def emit_chunk(b, c):
            block_scan_chunk(b, c)
            if b == 0:
                O("scalar", lambda e, c=c: e.dma_start(
                    out=h_ysave[128 * c:128 * (c + 1), :],
                    in_=s_ybuf[:, 0:T]),
                  reads=["ybuf"], writes=["h_ysave"], dma="d_sp")
                O("scalar", lambda e, c=c: e.dma_start(
                    out=h_dsave[128 * c:128 * (c + 1), :],
                    in_=s_delta[:, T * (c % 4) if False else T * c:
                        T * c + T]),
                  reads=["dl%d" % c], writes=["h_dsave"], dma="d_sp")
                O("scalar", lambda e, c=c: e.dma_start(
                    out=h_srsave[128 * c:128 * (c + 1), :],
                    in_=s_sres[:, T * c:T * (c + 1)]),
                  reads=["sres%d" % c], writes=["h_srsave"], dma="d_sp")
            else:
                block_gate_chunk(b, c)

        head_pre(0)
        for mg in range(16):
            head_mg(0, mg)
        head_post(0)
        HEAD_DMA_SCHED = {p - 1: lst for p, lst in HEAD_MG_SCHED.items()}
        for b in range(NB):
            nxt = b + 1 < NB
            for c in range(CD):
                emit_chunk(b, c)
                if WIN_PREFETCH and b >= 2 and c <= 2:
                    tail_mm_dma(b - 1, c + 1)
                if b >= 2 and c <= 3:
                    tail_mm(b - 1, (b - 1) % 2, c,
                            dma=not WIN_PREFETCH or c == 0 and False)
                if b >= 2 and c == 4:
                    tail_fin(b - 1, (b - 1) % 2)
                if nxt and INTERLEAVE_HEAD >= 1 and c == 5:
                    head_pre(b + 1)
                if nxt and INTERLEAVE_HEAD >= 2:
                    if c in HEAD_MG_SCHED:
                        for mg, slot in HEAD_MG_SCHED[c]:
                            head_mg(b + 1, mg, slot, dma=not WIN_PREFETCH)
                    if WIN_PREFETCH and c in HEAD_DMA_SCHED:
                        for mg, slot in HEAD_DMA_SCHED[c]:
                            head_mg_dma(b + 1, mg, slot)
                if nxt and INTERLEAVE_HEAD >= 3 and c == CD - 1:
                    head_post(b + 1)
                if WIN_PREFETCH and b >= 1 and c == CD - 1:
                    tail_mm_dma(b, 0)
            if nxt:
                if INTERLEAVE_HEAD < 1:
                    head_pre(b + 1)
                if INTERLEAVE_HEAD < 2:
                    for mg in range(16):
                        head_mg(b + 1, mg)
                if INTERLEAVE_HEAD < 3:
                    head_post(b + 1)
        for mg in range(4):
            tail_mm(NB - 1, (NB - 1) % 2, mg,
                    dma=(mg > 0 or not WIN_PREFETCH))
        tail_fin(NB - 1, (NB - 1) % 2)

        for c in range(CD):
            O("scalar", lambda e, c=c: e.activation(
                s_tmp[:, DS * c:DS * (c + 1)], s_acols[:, DS * c:DS * (c + 1)],
                AT.Exp, scale=s_dssum[:, c:c + 1]),
              reads=["acols", "dssum"], writes=["dtot"])
        O("scalar", lambda e: e.dma_start(
            out=cc_in[:, 0:CD * DS], in_=s_tmp[:]),
          reads=["dtot"], writes=["ccin_h"], dma="d_sp")
        O("scalar", lambda e: e.dma_start(
            out=cc_in[:, CD * DS:2 * CD * DS], in_=s_scol[:]),
          reads=["scolA"], writes=["ccin_h"], dma="d_sp")
        O("gpsimd", lambda e: e.collective_compute(
            "AllGather", OP.bypass, replica_groups=[list(range(cfg.NC))],
            ins=[cc_in.opt()], outs=[cc_out.opt()]),
          reads=["ccin_h"], writes=["ccout_h"])
        O("vector", lambda e: e.memset(s_sin[:], 0.0), writes=["sin"])
        for j in range(cfg.NC):
            gs = "gath0"
            off = 0
            O("sync", lambda e, j=j, off=off: e.dma_start(
                out=s_gath[:, off:off + 2 * CD * DS],
                in_=cc_out[128 * j:128 * (j + 1), :]),
              reads=["ccout_h"], writes=[gs], dma="d_x")
            O("vector", lambda e, j=j, off=off: e.tensor_scalar(
                s_tmp[:], s_gath[:, off:off + CD * DS],
                s_chain[:, 2 * j:2 * j + 1], s_chain[:, 2 * j + 1:2 * j + 2],
                OP.mult, OP.add),
              reads=[gs, "chain"], writes=["dtp"])
            O("vector", lambda e: e.tensor_mul(s_sin[:], s_sin[:], s_tmp[:]),
              reads=["dtp", "sin"], writes=["sin"])
            O("vector", lambda e, j=j, off=off: e.scalar_tensor_tensor(
                s_sin[:], s_gath[:, off + CD * DS:off + 2 * CD * DS],
                s_chain[:, 2 * j:2 * j + 1], s_sin[:], OP.mult, OP.add),
              reads=[gs, "chain", "sin"], writes=["sin"])
        for n in range(DS):
            pb = "pmm%d" % (2 + (n % 2))
            O("tensor", lambda e, n=n: e.matmul(
                p_mm[2 + (n % 2)][:, 0:T],
                s_selbf[:, 128 * (16 + n):128 * (17 + n)],
                s_bc0[0:32, 0:T], start=True, stop=True),
              reads=["bc0", "selbf"], writes=[pb])
            O("scalar", lambda e, n=n, pb=pb: e.copy(
                s_crep[:, T * n:T * (n + 1)], p_mm[2 + (n % 2)][:, 0:T]),
              reads=[pb, "crepA"], writes=["crepA"])
        for c in range(CD):
            O("scalar", lambda e, c=c: e.dma_start(
                out=s_e1[:, 0:T], in_=h_dsave[128 * c:128 * (c + 1), :]),
              reads=["h_dsave", "e10"], writes=["e10"], dma="d_r")
            O("scalar", lambda e, c=c: e.dma_start(
                out=s_ybuf[:, 0:T], in_=h_ysave[128 * c:128 * (c + 1), :]),
              reads=["h_ysave", "ybuf"], writes=["ybuf"], dma="d_r")
            O("scalar", lambda e, c=c: e.dma_start(
                out=s_sres[:, T * c:T * (c + 1)],
                in_=h_srsave[128 * c:128 * (c + 1), :]),
              reads=["h_srsave", "sres%d" % c], writes=["sres%d" % c],
              dma="d_r")
            O("vector", lambda e: e.tensor_tensor_scan(
                s_cums[:, 0:T], s_onet[:, 0:T], s_e1[:, 0:T], 0.0,
                OP.mult, OP.add),
              reads=["e10", "onet"], writes=["cums"])
            for n in range(DS):
                O("scalar", lambda e, c=c, n=n: e.activation(
                    s_sp[:, T * n:T * (n + 1)], s_cums[:, 0:T], AT.Exp,
                    scale=s_acols[:, DS * c + n:DS * c + n + 1]),
                  reads=["cums", "acols", "spall"], writes=["spall"])
                O("vector", lambda e, c=c, n=n: e.tensor_scalar(
                    s_sp[:, T * n:T * (n + 1)], s_sp[:, T * n:T * (n + 1)],
                    s_sin[:, DS * c + n:DS * c + n + 1], None, OP.mult),
                  reads=["spall", "sin"], writes=["spall"])
                O("vector", lambda e, n=n: e.tensor_mul(
                    s_sp[:, T * n:T * (n + 1)], s_sp[:, T * n:T * (n + 1)],
                    s_crep[:, T * n:T * (n + 1)]),
                  reads=["spall", "crepA"], writes=["spall"])
            O("vector", lambda e: e.tensor_reduce(
                s_e1[:, 0:T],
                s_sp[:, 0:DS * T].rearrange("p (n t) -> p t n", n=DS),
                axis=AX, op=OP.add),
              reads=["spall"], writes=["e10"])
            O("vector", lambda e: e.tensor_add(
                s_ybuf[:, 0:T], s_ybuf[:, 0:T], s_e1[:, 0:T]),
              reads=["e10", "ybuf"], writes=["ybuf"])
            block_gate_chunk(0, c)
        for i in range(TT):
            O("sync", lambda e, i=i: e.dma_start(
                out=s_x[:, DM * i:DM * (i + 1)],
                in_=x_in[128 * i:128 * (i + 1), :]),
              writes=["xst0"], dma="d_x")
        for mg in range(4):
            tail_mm(0, 0, mg)
        tail_fin(0, 0)

        mam.close()
        P.barrier()
        if stop == "mamba":
            _end()
            return nc

        ffn = ExitStack()
        s_xq = sb(ffn, "s_xq", (128, CM * LC), BF16)
        s_strip = sb(ffn, "s_strip", (128, 24576), BF16)
        s_f2 = sb(ffn, "s_f2", (128, 2 * 1024), F32)
        ffnA0 = ExitStack()
        s_f1 = sb(ffnA0, "s_f1", (128, 2 * 1024), F32)
        s_i32 = sb(ffnA0, "s_i32", (128, 2 * 1024), I32)
        s_fbf = sb(ffnA0, "s_fbf", (128, 2 * 1024), BF16)

        import sys
        print("SBUF remaining after ffn allocs:", nc.sbuf_bytes_remaining,
              file=sys.stderr)

        def colmax_to_scales(ncol_src, ccm_slot, scl_slot, nvals):
            O("tensor", lambda e, nvals=nvals: e.transpose(
                p_tf[0][0:nvals, :], s_m1[:, 0:nvals], s_id32[:]),
              reads=["m1", "id32"], writes=["ptf0"])
            O("vector", lambda e, nvals=nvals: e.reduce_max(
                s_sclr[0:nvals, 0:1], p_tf[0][0:nvals, 0:128], axis=AX),
              reads=["ptf0"], writes=["sclrA"])
            O("scalar", lambda e, nvals=nvals, ccm_slot=ccm_slot: e.dma_start(
                out=ccm_in[0, ccm_slot:ccm_slot + nvals],
                in_=s_sclr[0:nvals, 0:1]),
              reads=["sclrA", "ccmin_h"], writes=["ccmin_h"], dma="d_sp")
            O("gpsimd", lambda e: e.collective_compute(
                "AllReduce", OP.max, replica_groups=[list(range(cfg.NC))],
                ins=[ccm_in.opt()], outs=[ccm_out.opt()]),
              reads=["ccmin_h"], writes=["ccmout_h"])
            O("sync", lambda e: e.dma_start(
                out=s_sclr[0:1, 0:8], in_=ccm_out[0:1, :]),
              reads=["ccmout_h", "sclr8"], writes=["sclr8"], dma="d_x")
            O("vector", lambda e, nvals=nvals, ccm_slot=ccm_slot:
              e.tensor_scalar(
                  s_sclr[0:1, 8:8 + nvals],
                  s_sclr[0:1, ccm_slot:ccm_slot + nvals],
                  RQMAX, 1e-12, OP.mult, OP.max),
              reads=["sclr8"], writes=["srow"])
            O("vector", lambda e, nvals=nvals: e.reciprocal(
                s_sclr[0:1, 8 + nvals:8 + 2 * nvals], s_sclr[0:1, 8:8 + nvals]),
              reads=["srow"], writes=["irow"])
            O("vector", lambda e, nvals=nvals: e.tensor_scalar_mul(
                s_sclr[0:1, 8 + 2 * nvals:8 + 3 * nvals],
                s_sclr[0:1, 8:8 + nvals], -32768.0),
              reads=["srow"], writes=["brow"])
            O("tensor", lambda e, nvals=nvals: e.matmul(
                p_tf[1][:, 0:3 * nvals], s_ones32[:],
                s_sclr[0:1, 8:8 + 3 * nvals], start=True, stop=True),
              reads=["srow", "irow", "brow", "ones32"], writes=["ptf1"])
            O("scalar", lambda e, nvals=nvals, scl_slot=scl_slot: e.copy(
                s_scl[:, scl_slot:scl_slot + 3 * nvals],
                p_tf[1][:, 0:3 * nvals]),
              reads=["ptf1"], writes=["scl"])

        O("vector", lambda e: e.tensor_copy(s_m1[:, 0:1], s_xmax[:]),
          reads=["xmax"], writes=["m1"])
        colmax_to_scales(1, 0, 0, 1)
        SC_SX, SC_IX, SC_BX = 0, 1, 2
        if stop == "scl":
            ffnA0.close(); ffn.close(); _end(); return nc

        ntile = LC // 128
        for w in range(ntile):
            sl = w % 2
            f1s = s_f1[:, 1024 * sl:1024 * (sl + 1)]
            i32s = s_i32[:, 1024 * sl:1024 * (sl + 1)]
            fbfs = s_fbf[:, 1024 * sl:1024 * (sl + 1)]
            O("sync", lambda e, w=w, f1s=f1s: e.dma_start(
                out=f1s, in_=h_x1n[128 * w:128 * (w + 1), :]),
              reads=["h_x1n", "f1.%d" % sl], writes=["f1.%d" % sl], dma="d_r")
            O("scalar", lambda e, f1s=f1s, i32s=i32s: e.activation(
                i32s, f1s, AT.Identity,
                scale=s_scl[:, SC_IX:SC_IX + 1]),
              reads=["f1.%d" % sl, "scl", "i32.%d" % sl],
              writes=["i32.%d" % sl])
            O("scalar", lambda e, i32s=i32s, fbfs=fbfs: e.activation(
                fbfs, i32s, AT.Identity,
                scale=s_scl[:, SC_SX:SC_SX + 1]),
              reads=["i32.%d" % sl, "scl", "fbf.%d" % sl],
              writes=["fbf.%d" % sl])
            for kc in range(CM):
                pk = "ptr%d" % (kc % 2)
                O("tensor", lambda e, kc=kc, fbfs=fbfs: e.transpose(
                    p_tr[kc % 2][:, 0:128], fbfs[:, 128 * kc:128 * (kc + 1)],
                    s_idbf[:]),
                  reads=["fbf.%d" % sl, "idbf"], writes=[pk])
                O("scalar", lambda e, kc=kc, w=w, pk=pk: e.copy(
                    s_xq[:, LC * kc + 128 * w:LC * kc + 128 * (w + 1)],
                    p_tr[kc % 2][:, 0:128]),
                  reads=[pk], writes=["xq%d" % kc])

        if DBG_HBM:
            O("scalar", lambda e: e.dma_start(out=h_dbg_xq[:, :], in_=s_xq[:]),
              reads=[*("xq%d" % kc for kc in range(CM))], dma="d_sp")
        if stop == "xq":
            ffnA0.close(); ffn.close(); _end(); return nc
        P.dma_drain()
        P.barrier()
        ffnA0.close()
        s_stripG = sb(ffn, "s_stripG", (128, 24576), BF16)

        def mm_pass(h_wsrc, wkey, h_dst, dkey, maxcol, stb, spfx):
            parts = MM_PARTS
            if "strips" in parts:
                for k in range(8):
                    O("sync", lambda e, k=k, stb=stb: e.dma_start_transpose(
                        stb[:, 3072 * k:3072 * (k + 1)],
                        h_wsrc[:, 128 * k:128 * (k + 1)]),
                      reads=[wkey], writes=["%s%d" % (spfx, k)], dma="d_w")
            for tb2 in range(NTB // 2):
                for og in range(12):
                    for oi in range(2):
                        o = 2 * og + oi
                        pi = 2 * (og % 2) + oi
                        pm = "pmm%d" % pi
                        if "mm" in parts:
                            fr = []
                            for k in range(8):
                                fr.append((
                                    (lambda e, k=k, o=o, tb2=tb2, pi=pi,
                                     stb=stb: e.matmul(
                                        p_mm[pi][:, 0:512],
                                        stb[:, 3072 * k + 128 * o:
                                            3072 * k + 128 * (o + 1)],
                                        s_xq[:, LC * k + 512 * tb2:
                                             LC * k + 512 * (tb2 + 1)],
                                        start=(k == 0), stop=(k == 7))),
                                    ["%s%d" % (spfx, k), "xq%d" % k]))
                            P.mm_group(fr, writes=[pm])
                        fsl = (tb2 * 24 + o) % 4
                        f2s = s_f2[:, 512 * fsl:512 * (fsl + 1)]
                        if "evac" in parts:
                            O("scalar", lambda e, pi=pi, pm=pm, f2s=f2s: e.copy(
                                f2s, p_mm[pi][:, 0:512]),
                              reads=[pm, "f2.%d" % fsl], writes=["f2.%d" % fsl])
                        if "reduce" in parts:
                            O("vector", lambda e, f2s=f2s: e.tensor_reduce(
                                s_stat[:, 13:14], f2s, axis=AX,
                                op=OP.max, apply_absolute_value=True),
                              reads=["f2.%d" % fsl], writes=["ffm"])
                            O("vector", lambda e, maxcol=maxcol: e.tensor_max(
                                s_m1[:, maxcol:maxcol + 1],
                                s_m1[:, maxcol:maxcol + 1], s_stat[:, 13:14]),
                              reads=["ffm", "m1"], writes=["m1"])
                        if "spill" in parts:
                            O("scalar", lambda e, o=o, tb2=tb2, f2s=f2s:
                              e.dma_start(
                                out=h_dst[128 * o:128 * (o + 1),
                                          512 * tb2:512 * (tb2 + 1)],
                                in_=f2s),
                              reads=["f2.%d" % fsl],
                              writes=["%s.%d.%d" % (dkey, o, 2 * tb2),
                                      "%s.%d.%d" % (dkey, o, 2 * tb2 + 1)],
                              dma="d_sp")

        O("vector", lambda e: e.memset(s_m1[:, 0:4], 0.0),
          reads=["m1"], writes=["m1"])
        mm_pass(h_wu, "h_wu", h_up, "h_up", 0, s_strip, "strip")
        if stop == "up":
            ffn.close(); _end(); return nc
        mm_pass(h_wg, "h_wg", h_gate, "h_gate", 1, s_stripG, "stripG")

        P.dma_drain()
        P.barrier()
        ffn.close()
        ffn = ExitStack()
        NSL = 8
        s_f1B = sb(ffn, "s_f1b", (128, NSL * 1024), F32)
        s_f2B = sb(ffn, "s_f2b", (128, NSL * 1024), F32)
        s_f3B = sb(ffn, "s_f3b", (128, NSL * 1024), F32)
        s_i32B = sb(ffn, "s_i32b", (128, NSL * 1024), I32)

        colmax_to_scales(2, 1, 3, 2)
        SC_SU, SC_SG, SC_IU, SC_IG, SC_BU, SC_BG = 3, 4, 5, 6, 7, 8

        ntile2 = CI * (LC // 1024)
        ncl = LC // 1024
        O("vector", lambda e: e.memset(s_m1[:, 0:1], 0.0),
          reads=["m1"], writes=["m1"])
        for w in range(ntile2):
            oi_ = w % CI
            cl_i = w // CI
            r = oi_ * 128
            cl = cl_i * 1024
            tb0 = 4 * cl_i
            sl = w % NSL
            f1s = s_f1B[:, 1024 * sl:1024 * (sl + 1)]
            f2s = s_f2B[:, 1024 * sl:1024 * (sl + 1)]
            f3s = s_f3B[:, 1024 * sl:1024 * (sl + 1)]
            i32s = s_i32B[:, 1024 * sl:1024 * (sl + 1)]
            k1, k2, k3, ki = ("f1.%d" % sl, "f2.%d" % sl, "f3.%d" % sl,
                              "i32.%d" % sl)
            O("sync", lambda e, r=r, cl=cl, f1s=f1s: e.dma_start(
                out=f1s, in_=h_gate[r:r + 128, cl:cl + 1024]),
              reads=[k1] + ["h_gate.%d.%d" % (oi_, tb0 + j)
                            for j in range(4)],
              writes=[k1], dma="d_r")
            O("scalar", lambda e, f1s=f1s, i32s=i32s: e.activation(
                i32s, f1s, AT.Identity, scale=s_scl[:, SC_IG:SC_IG + 1]),
              reads=[k1, "scl", ki], writes=[ki])
            O("scalar", lambda e, f1s=f1s, i32s=i32s: e.activation(
                f1s, i32s, AT.Identity, scale=s_scl[:, SC_SG:SC_SG + 1]),
              reads=[ki, "scl", k1], writes=[k1])
            O("scalar", lambda e, f1s=f1s, f2s=f2s: e.activation(
                f2s, f1s, AT.Sigmoid),
              reads=[k1, k2], writes=[k2])
            O("scalar", lambda e, f2s=f2s, i32s=i32s: e.activation(
                i32s, f2s, AT.Identity, scale=65536.0),
              reads=[k2, ki], writes=[ki])
            O("vector", lambda e, f2s=f2s, i32s=i32s: e.tensor_scalar(
                f2s, i32s, 65535.0, R65536, OP.min, OP.mult),
              reads=[ki, k2], writes=[k2])
            O("vector", lambda e, f1s=f1s, f2s=f2s, f3s=f3s: e.tensor_mul(
                f3s, f1s, f2s),
              reads=[k1, k2, k3], writes=[k3])
            O("vector", lambda e, f3s=f3s: e.tensor_reduce(
                s_stat[:, 13:14], f3s, axis=AX, op=OP.max,
                apply_absolute_value=True),
              reads=[k3], writes=["ffm"])
            O("vector", lambda e: e.tensor_max(
                s_m1[:, 0:1], s_m1[:, 0:1], s_stat[:, 13:14]),
              reads=["ffm", "m1"], writes=["m1"])
            O("scalar", lambda e, r=r, cl=cl, f3s=f3s: e.dma_start(
                out=h_g2[r:r + 128, cl:cl + 1024], in_=f3s),
              reads=[k3],
              writes=["h_g2.%d.%d" % (oi_, tb0 + j) for j in range(4)],
              dma="d_sp")
        colmax_to_scales(1, 3, 9, 1)
        SC_S2, SC_I2, SC_B2 = 9, 10, 11
        if stop == "g2":
            ffn.close(); _end(); return nc

        O("vector", lambda e: e.memset(s_m1[:, 0:1], 0.0),
          reads=["m1"], writes=["m1"])
        for w in range(ntile2):
            oi_ = w % CI
            cl_i = w // CI
            r = oi_ * 128
            cl = cl_i * 1024
            tb0 = 4 * cl_i
            sl = w % NSL
            f1s = s_f1B[:, 1024 * sl:1024 * (sl + 1)]
            f2s = s_f2B[:, 1024 * sl:1024 * (sl + 1)]
            f3s = s_f3B[:, 1024 * sl:1024 * (sl + 1)]
            i32s = s_i32B[:, 1024 * sl:1024 * (sl + 1)]
            k1, k2, k3, ki = ("f1.%d" % sl, "f2.%d" % sl, "f3.%d" % sl,
                              "i32.%d" % sl)
            O("sync", lambda e, r=r, cl=cl, f1s=f1s: e.dma_start(
                out=f1s, in_=h_g2[r:r + 128, cl:cl + 1024]),
              reads=[k1] + ["h_g2.%d.%d" % (oi_, tb0 + j) for j in range(4)],
              writes=[k1], dma="d_r")
            O("sync", lambda e, r=r, cl=cl, f2s=f2s: e.dma_start(
                out=f2s, in_=h_up[r:r + 128, cl:cl + 1024]),
              reads=[k2] + ["h_up.%d.%d" % (oi_, tb0 + j) for j in range(4)],
              writes=[k2], dma="d_r")
            O("scalar", lambda e, f1s=f1s, i32s=i32s: e.activation(
                i32s, f1s, AT.Identity, scale=s_scl[:, SC_I2:SC_I2 + 1]),
              reads=[k1, "scl", ki], writes=[ki])
            O("scalar", lambda e, f1s=f1s, i32s=i32s: e.activation(
                f1s, i32s, AT.Identity, scale=s_scl[:, SC_S2:SC_S2 + 1]),
              reads=[ki, "scl", k1], writes=[k1])
            O("scalar", lambda e, f2s=f2s, i32s=i32s: e.activation(
                i32s, f2s, AT.Identity, scale=s_scl[:, SC_IU:SC_IU + 1]),
              reads=[k2, "scl", ki], writes=[ki])
            O("scalar", lambda e, f2s=f2s, i32s=i32s: e.activation(
                f2s, i32s, AT.Identity, scale=s_scl[:, SC_SU:SC_SU + 1]),
              reads=[ki, "scl", k2], writes=[k2])
            O("vector", lambda e, f1s=f1s, f2s=f2s, f3s=f3s: e.tensor_mul(
                f3s, f1s, f2s),
              reads=[k1, k2, k3], writes=[k3])
            O("vector", lambda e, f3s=f3s: e.tensor_reduce(
                s_stat[:, 13:14], f3s, axis=AX, op=OP.max,
                apply_absolute_value=True),
              reads=[k3], writes=["ffm"])
            O("vector", lambda e: e.tensor_max(
                s_m1[:, 0:1], s_m1[:, 0:1], s_stat[:, 13:14]),
              reads=["ffm", "m1"], writes=["m1"])
            O("scalar", lambda e, r=r, cl=cl, f3s=f3s: e.dma_start(
                out=h_gate[r:r + 128, cl:cl + 1024], in_=f3s),
              reads=[k3],
              writes=["h_o.%d.%d" % (oi_, tb0 + j) for j in range(4)]
              + ["h_gate.%d.%d" % (oi_, tb0 + j) for j in range(4)],
              dma="d_sp")
        P.dma_drain()
        P.barrier()
        ffn.close()
        ffn = ExitStack()
        s_stripC = sb(ffn, "s_strip2", (128, 24576), BF16)
        s_f1C = sb(ffn, "s_f1c", (128, 4 * 512), F32)
        s_i32C = sb(ffn, "s_i32c", (128, 4 * 512), I32)
        s_f3C = sb(ffn, "s_f3c", (128, 2 * 1024), F32)
        s_obf = sb(ffn, "s_obf", (128, CI * 512), BF16)
        s_dfm = sb(ffn, "s_dfm", (128, CM * 512), F32)
        s_x2 = sb(ffn, "s_x2", (128, 2 * DM), F32)
        s_sq2 = sb(ffn, "s_sq2", (128, DM), F32)

        colmax_to_scales(1, 4, 12, 1)
        SC_SO, SC_IO, SC_BO = 12, 13, 14
        if stop == "o":
            ffn.close(); _end(); return nc

        for k in range(CI):
            O("sync", lambda e, k=k: e.dma_start_transpose(
                s_stripC[:, 1024 * k:1024 * (k + 1)],
                h_wd[:, 128 * k:128 * (k + 1)]),
              reads=["h_wd"], writes=["dstrip%d" % k], dma="d_w")
        last_out = [None]
        for tb2 in range(NTB // 2):
            for k in range(CI):
                sl = k % 4
                f1s = s_f1C[:, 512 * sl:512 * (sl + 1)]
                i32s = s_i32C[:, 512 * sl:512 * (sl + 1)]
                k1, ki = "f1.%d" % sl, "i32.%d" % sl
                O("sync", lambda e, k=k, tb2=tb2, f1s=f1s: e.dma_start(
                    out=f1s,
                    in_=h_gate[128 * k:128 * (k + 1),
                               512 * tb2:512 * (tb2 + 1)]),
                  reads=["h_o.%d.%d" % (k, 2 * tb2),
                         "h_o.%d.%d" % (k, 2 * tb2 + 1), k1],
                  writes=[k1], dma="d_r")
                O("scalar", lambda e, f1s=f1s, i32s=i32s: e.activation(
                    i32s, f1s, AT.Identity,
                    scale=s_scl[:, SC_IO:SC_IO + 1]),
                  reads=[k1, "scl", ki], writes=[ki])
                O("scalar", lambda e, k=k, i32s=i32s: e.activation(
                    s_obf[:, 512 * k:512 * (k + 1)], i32s,
                    AT.Identity, scale=s_scl[:, SC_SO:SC_SO + 1]),
                  reads=[ki, "scl"], writes=["ob%d" % k])
            for m in range(CM):
                pm = "pmm%d" % (m % 2)
                fr = []
                for k in range(CI):
                    fr.append((
                        (lambda e, k=k, m=m: e.matmul(
                            p_mm[m % 2][:, 0:512],
                            s_stripC[:, 1024 * k + 128 * m:
                                    1024 * k + 128 * (m + 1)],
                            s_obf[:, 512 * k:512 * (k + 1)],
                            start=(k == 0), stop=(k == CI - 1))),
                        ["dstrip%d" % k, "ob%d" % k]))
                P.mm_group(fr, writes=[pm])
                O("scalar", lambda e, m=m, pm=pm: e.copy(
                    s_dfm[:, 512 * m:512 * (m + 1)], p_mm[m % 2][:, 0:512]),
                  reads=[pm], writes=["dfm%d" % m])
            for i in range(4):
                sl = i % 2
                tb = 2 * tb2 + i // 2
                f3s = s_f3C[:, 1024 * sl:1024 * (sl + 1)]
                k3 = "f3.%d" % sl
                O("sync", lambda e, tb2=tb2, i=i, f3s=f3s: e.dma_start(
                    out=f3s, in_=h_x1n[512 * tb2 + 128 * i:
                                       512 * tb2 + 128 * (i + 1), :]),
                  reads=["h_x1n", k3], writes=[k3], dma="d_x")
                for m in range(CM):
                    pf = "ptf%d" % (m % 2)
                    O("tensor", lambda e, m=m, i=i: e.transpose(
                        p_tf[m % 2][:],
                        s_dfm[:, 512 * m + 128 * i:512 * m + 128 * (i + 1)],
                        s_id32[:]),
                      reads=["dfm%d" % m, "id32"], writes=[pf])
                    O("vector", lambda e, m=m, sl=sl, pf=pf, f3s=f3s:
                      e.tensor_add(
                          s_x2[:, DM * sl + 128 * m:DM * sl + 128 * (m + 1)],
                          p_tf[m % 2][:], f3s[:, 128 * m:128 * (m + 1)]),
                      reads=[pf, k3], writes=["x2t%d" % sl])
                x2t = s_x2[:, DM * sl:DM * (sl + 1)]
                O("scalar", lambda e, x2t=x2t: e.activation(
                    s_sq2[:], x2t, AT.Square, accum_out=s_stat[:, 3:4]),
                  reads=["x2t%d" % sl, "sq2"], writes=["lst0", "sq2"])
                O("vector", lambda e, x2t=x2t: e.reduce_sum(
                    s_stat[:, 4:5], x2t, axis=AX),
                  reads=["x2t%d" % sl], writes=["lst1"])
                O("vector", lambda e: e.tensor_scalar_mul(
                    s_stat[:, 5:6], s_stat[:, 4:5], 1.0 / DM),
                  reads=["lst1"], writes=["lmean"])
                O("vector", lambda e: e.tensor_mul(
                    s_stat[:, 6:7], s_stat[:, 5:6], s_stat[:, 5:6]),
                  reads=["lmean"], writes=["lm2"])
                O("vector", lambda e: e.scalar_tensor_tensor(
                    s_stat[:, 7:8], s_stat[:, 3:4], 1.0 / DM,
                    s_stat[:, 6:7], OP.mult, OP.subtract),
                  reads=["lst0", "lm2"], writes=["lvar"])
                O("scalar", lambda e: e.activation(
                    s_stat[:, 9:10], s_stat[:, 7:8], AT.Sqrt, bias=EPS),
                  reads=["lvar", "kconst"], writes=["lsd"])
                O("vector", lambda e: e.reciprocal(
                    s_stat[:, 10:11], s_stat[:, 9:10]),
                  reads=["lsd"], writes=["lrs"])
                O("vector", lambda e: e.tensor_scalar_mul(
                    s_stat[:, 11:12], s_stat[:, 5:6], -1.0),
                  reads=["lmean"], writes=["lnm"])
                O("vector", lambda e, x2t=x2t: e.tensor_scalar(
                    x2t, x2t, s_stat[:, 11:12], s_stat[:, 10:11],
                    OP.add, OP.mult),
                  reads=["x2t%d" % sl, "lnm", "lrs"], writes=["x2t%d" % sl])
                last_out[0] = O("scalar", lambda e, tb2=tb2, i=i,
                                x2t=x2t: e.dma_start(
                                    out=out[512 * tb2 + 128 * i:
                                            512 * tb2 + 128 * (i + 1), :],
                                    in_=x2t),
                                reads=["x2t%d" % sl], writes=["outw"],
                                dma="d_o")
        P._dep_wait("sync", last_out[0])
        P.op("sync", lambda e: e.nop())

        ffn.close()
        _end()

    return nc



def make_inmaps(cfg, inputs):
    x = np.ascontiguousarray(np.asarray(inputs["L0_attn_x"], np.float32))
    B, L, _ = x.shape
    xf = x.reshape(B * L, DM)
    lc = cfg.LC
    conv_w = np.asarray(inputs["L0_attn_conv_w"], np.float32)[:, 0, :]
    conv_b = np.asarray(inputs["L0_attn_conv_b"], np.float32)
    Dp = np.asarray(inputs["L0_attn_D"], np.float32)
    dt_b = np.asarray(inputs["L0_attn_dt_proj_b"], np.float32)
    A_log = np.asarray(inputs["L0_attn_A_log"], np.float32)
    norm_w = np.asarray(inputs["L0_attn_norm_w"], np.float32)

    chv = np.zeros((128, 7 * CD), np.float32)
    alog = np.zeros((128, CD * DS), np.float32)
    for c in range(CD):
        rows = slice(128 * c, 128 * (c + 1))
        chv[:, 7 * c:7 * c + 4] = conv_w[rows, :]
        chv[:, 7 * c + 4] = conv_b[rows]
        chv[:, 7 * c + 5] = Dp[rows]
        chv[:, 7 * c + 6] = dt_b[rows]
        alog[:, DS * c:DS * (c + 1)] = A_log[rows, :]
    normw_rep = np.ascontiguousarray(
        np.broadcast_to(norm_w[None, :], (128, DM)))
    identity = np.eye(128, dtype=np.float32)
    ones_row = np.ones((1, 128), np.float32)

    common = dict(
        w_in=np.ascontiguousarray(np.asarray(inputs["L0_attn_in_proj_w"],
                                             np.float32)),
        w_out=np.ascontiguousarray(np.asarray(inputs["L0_attn_out_proj_w"],
                                              np.float32)),
        w_xp=np.ascontiguousarray(np.asarray(inputs["L0_attn_x_proj_w"],
                                             np.float32)),
        w_dt=np.ascontiguousarray(np.asarray(inputs["L0_attn_dt_proj_w"],
                                             np.float32)),
        w_g=np.ascontiguousarray(np.asarray(inputs["L0_ffn_w_gate"],
                                            np.float32)),
        w_u=np.ascontiguousarray(np.asarray(inputs["L0_ffn_w_up"],
                                            np.float32)),
        w_d=np.ascontiguousarray(np.asarray(inputs["L0_ffn_w_down"],
                                            np.float32)),
        chvec=chv, alog=alog, normw=normw_rep, ident=identity,
        ones_r=ones_row,
    )
    in_maps = []
    cores_per_batch = cfg.NC // B if cfg.NC >= B else 1
    for core in range(cfg.NC):
        start = core * lc
        first = (core % cores_per_batch) == 0
        if first:
            halo = np.zeros((3, DM), np.float32)
        else:
            halo = xf[start - 3:start]
        g0 = (core // cores_per_batch) * cores_per_batch
        chain = np.zeros((128, 16), np.float32)
        for j in range(cfg.NC):
            if g0 <= j < core:
                chain[:, 2 * j] = 1.0
                chain[:, 2 * j + 1] = 0.0
            else:
                chain[:, 2 * j] = 0.0
                chain[:, 2 * j + 1] = 1.0
        m = dict(common)
        m["x_in"] = np.ascontiguousarray(xf[start:start + lc])
        m["xh3"] = np.ascontiguousarray(halo)
        m["chain_ab"] = chain
        in_maps.append(m)
    return in_maps


_CFG = CFG()
_NC_CACHE = {}


def kernel(**inputs):
    cfg = _CFG
    key = (cfg.LC, cfg.T, cfg.GP, cfg.NC)
    if key not in _NC_CACHE:
        _NC_CACHE[key] = build_nc(cfg)
    nc = _NC_CACHE[key]
    in_maps = make_inmaps(cfg, inputs)
    res = run_bass_kernel_spmd(nc, in_maps, core_ids=list(range(cfg.NC)))
    outs = [res.results[c]["out"] for c in range(cfg.NC)]
    full = np.concatenate(outs, axis=0)
    x = np.asarray(inputs["L0_attn_x"])
    return full.reshape(x.shape).astype(np.float32)


# version 21
# speedup vs baseline: 1.0635x; 1.0635x over previous
import numpy as np
import concourse.bass as bass
import concourse.mybir as mybir
from concourse.bass_utils import run_bass_kernel_spmd

F32 = mybir.dt.float32
BF16 = mybir.dt.bfloat16
I32 = mybir.dt.int32
AT = mybir.ActivationFunctionType
OP = mybir.AluOpType
AX = mybir.AxisListType.X

ENGS = ("sync", "vector", "scalar", "tensor", "gpsimd")
MM_PARTS = ("strips", "mm", "reduce", "evac", "spill")
DBG_HBM = False
INTERLEAVE_HEAD = 3
XPROJ_IN_MG = False
WIN_PREFETCH = False
HEAD_MG_SCHED = {
    6: [(0, 0), (1, 1)],
    7: [(8, 0), (9, 1)],
    8: [(2, 0), (3, 1)],
    9: [(10, 0), (11, 1)],
    10: [(4, 0)],
    11: [(5, 1), (12, 0)],
    12: [(13, 1)],
    13: [(6, 0), (14, 1)],
    15: [(7, 0), (15, 1)],
}
DBG_HBM_SET = {"h_x1n", "h_ysave", "h_dbg_u", "h_dbg_xi", "h_dbg_xdbl",
               "h_up", "h_g2", "h_dsave", "h_wu", "h_dbg_xq"}
EPS = 1e-5
RQMAX = float(np.float32(1.0) / np.float32(32767.0))
R65536 = float(np.float32(1.0) / np.float32(65536.0))
DM, DI, DS, DTR = 1024, 2048, 16, 64
CM = DM // 128
CD = DI // 128
CI = 3072 // 128


class CFG:
    def __init__(self, lc=4096, t=256, gp_planes=0, ncores=8):
        self.LC = lc
        self.T = t
        self.NB = lc // t
        self.TT = t // 128
        self.GP = gp_planes
        self.NC = ncores


class Mark:
    __slots__ = ("sem", "val", "eng")

    def __init__(self, eng):
        self.eng = eng
        self.sem = None
        self.val = None


class Prog:

    def __init__(self, dma_pools):
        self.q = {e: [] for e in ENGS}
        self.lastm = {e: None for e in ENGS}
        self.wr = {}
        self.rd = {}
        self.marks = []
        self.pools = {}
        for name, n in dma_pools.items():
            self.pools[name] = {"n": n, "i": 0,
                                "last": [None] * n, "cnt": [0] * n}

    def _dep_wait(self, eng, mark):
        self.q[eng].append(("w", mark))

    def _collect(self, reads, writes):
        deps = []
        for r in reads:
            d = self.wr.get(r)
            if d is not None:
                deps.append(d)
        for w in writes:
            d = self.wr.get(w)
            if d is not None:
                deps.append(d)
            deps.extend(self.rd.get(w, ()))
        out, seen = [], set()
        for d in deps:
            if id(d) not in seen:
                seen.add(id(d))
                out.append(d)
        return out

    def op(self, eng, fn, reads=(), writes=(), dma=None):
        for d in self._collect(reads, writes):
            self._dep_wait(eng, d)
        m = Mark(eng)
        if dma is not None:
            pool = self.pools[dma]
            slot = pool["i"] % pool["n"]
            pool["i"] += 1
            sem = "%s%d" % (dma, slot)
            prev = pool["last"][slot]
            if prev is not None:
                self._dep_wait(eng, prev)
            pool["cnt"][slot] += 16
            m.sem = sem
            m.val = pool["cnt"][slot]
            pool["last"][slot] = m
            self.q[eng].append(("i", fn, m, True))
        else:
            self.q[eng].append(("i", fn, m, False))
        self.lastm[eng] = m
        self.marks.append(m)
        for w in writes:
            self.wr[w] = m
            self.rd[w] = []
        for r in reads:
            self.rd.setdefault(r, []).append(m)
        return m

    def mm_group(self, fns_reads, writes=()):
        all_reads = []
        for _, rs in fns_reads:
            all_reads.extend(rs)
        for d in self._collect(all_reads, writes):
            self._dep_wait("tensor", d)
        m = Mark("tensor")
        for i, (fn, _) in enumerate(fns_reads):
            self.q["tensor"].append(
                ("i", fn, m if i == len(fns_reads) - 1 else None, False))
        self.lastm["tensor"] = m
        self.marks.append(m)
        for w in writes:
            self.wr[w] = m
            self.rd[w] = []
        for r in all_reads:
            self.rd.setdefault(r, []).append(m)
        return m

    def barrier(self):
        ms = {e: self.lastm[e] for e in ENGS if self.lastm[e] is not None}
        for e in ENGS:
            for pe, m in ms.items():
                if pe != e:
                    self._dep_wait(e, m)
        for k in list(self.wr):
            mk = self.wr[k]
            if mk.eng in ms:
                self.wr[k] = ms[mk.eng]
        self.rd = {k: [] for k in self.rd}
        return ms

    def dma_drain(self):
        for pool in self.pools.values():
            for m in pool["last"]:
                if m is not None:
                    for e in ENGS:
                        self._dep_wait(e, m)

    def finalize(self):
        needed = set()
        for e in ENGS:
            for it in self.q[e]:
                if it[0] == "w":
                    needed.add(id(it[1]))
        ecnt = {e: 0 for e in ENGS}
        self.final = {}
        for e in ENGS:
            out = []
            for it in self.q[e]:
                if it[0] == "i":
                    _, fn, m, is_dma = it
                    if m is None:
                        out.append(("i", fn, None, None, 0))
                    elif is_dma:
                        out.append(("i", fn, m, m.sem, 16))
                    elif id(m) in needed:
                        ecnt[e] += 1
                        m.val = ecnt[e]
                        m.sem = "_e_" + e
                        out.append(("i", fn, m, m.sem, 1))
                    else:
                        out.append(("i", fn, None, None, 0))
                else:
                    out.append(it)
            self.final[e] = out

    def run_engine(self, e, handle, sems):
        wm = {}
        for it in self.final[e]:
            if it[0] == "i":
                _, fn, m, semname, inc = it
                r = fn(handle)
                if m is not None:
                    r.then_inc(sems[semname], inc)
            else:
                m = it[1]
                if m.val is None:
                    continue
                if wm.get(m.sem, -1) >= m.val:
                    continue
                handle.wait_ge(sems[m.sem], m.val)
                wm[m.sem] = m.val


DMA_POOLS = {"d_x": 2, "d_w": 4, "d_c": 2, "d_sp": 8, "d_o": 2, "d_r": 6}


def _sem_names():
    names = ["_e_" + e for e in ENGS]
    for k, n in DMA_POOLS.items():
        names += ["%s%d" % (k, i) for i in range(n)]
    return names



def build_nc(cfg, dbg=(), stop=None):
    from contextlib import ExitStack
    LC, T, NB, TT = cfg.LC, cfg.T, cfg.NB, cfg.TT
    NTB = LC // 256
    nc = bass.Bass("TRN2", target_bir_lowering=False, debug=False)

    def din(name, shape, dt=F32):
        return nc.dram_tensor(name, list(shape), dt, kind="ExternalInput").ap()

    x_in = din("x_in", (LC, DM))
    xh3 = din("xh3", (3, DM))
    w_in = din("w_in", (DM, 2 * DI))
    w_out = din("w_out", (DI, DM))
    w_xp = din("w_xp", (DI, 96))
    w_dt = din("w_dt", (DTR, DI))
    w_g = din("w_g", (3072, DM))
    w_u = din("w_u", (3072, DM))
    w_d = din("w_d", (DM, 3072))
    chvec = din("chvec", (128, 7 * CD))
    alog = din("alog", (128, CD * DS))
    normw = din("normw", (128, DM))
    chain_ab = din("chain_ab", (128, 16))
    ident = din("ident", (128, 128))
    ones_r = din("ones_r", (1, 128))

    out = nc.dram_tensor("out", [LC, DM], F32, kind="ExternalOutput").ap()
    dbgo = {}
    for name, shape in dbg:
        dbgo[name] = nc.dram_tensor(name, list(shape), F32,
                                    kind="ExternalOutput").ap()

    def hbm(name, shape, dt):
        kind = "ExternalOutput" if (DBG_HBM and name in DBG_HBM_SET) \
            else "Internal"
        return nc.dram_tensor(name, list(shape), dt, kind=kind).ap()

    h_win = hbm("h_win", (16, 8, 128, 256), BF16)
    h_wout = hbm("h_wout", (4, 16, 128, 256), BF16)
    h_wu = hbm("h_wu", (3072, DM), BF16)
    h_wg = hbm("h_wg", (3072, DM), BF16)
    h_wd = hbm("h_wd", (DM, 3072), BF16)
    h_x1n = hbm("h_x1n", (LC, DM), F32)
    h_up = hbm("h_up", (3072, LC), F32)
    h_gate = hbm("h_gate", (3072, LC), F32)
    h_g2 = hbm("h_g2", (3072, LC), F32)
    h_ysave = hbm("h_ysave", (DI, T), F32)
    h_srsave = hbm("h_srsave", (DI, T), BF16)
    h_dsave = hbm("h_dsave", (DI, T), F32)
    h_dbg_u = hbm("h_dbg_u", (DI, T), F32)
    h_dbg_xi = hbm("h_dbg_xi", (DI, T), F32)
    h_dbg_xdbl = hbm("h_dbg_xdbl", (96, T), F32)
    h_dbg_xq = hbm("h_dbg_xq", (128, CM * LC), BF16)
    cc_in = hbm("cc_in", (128, 2 * CD * DS), F32)
    cc_out = hbm("cc_out", (cfg.NC * 128, 2 * CD * DS), F32)
    ccm_in = hbm("ccm_in", (1, 8), F32)
    ccm_out = hbm("ccm_out", (1, 8), F32)

    P = Prog(DMA_POOLS)

    def O(eng, fn, reads=(), writes=(), dma=None):
        return P.op(eng, fn, reads=reads, writes=writes, dma=dma)

    stack = ExitStack()
    sems = {}

    def _end():
        P.finalize()
        with nc.Block() as block:
            @block.sync
            def _(e):
                P.run_engine("sync", e, sems)

            @block.vector
            def _(e):
                P.run_engine("vector", e, sems)

            @block.scalar
            def _(e):
                P.run_engine("scalar", e, sems)

            @block.tensor
            def _(e):
                P.run_engine("tensor", e, sems)

            @block.gpsimd
            def _(e):
                P.run_engine("gpsimd", e, sems)

    with stack:
        for snm in _sem_names():
            sems[snm] = stack.enter_context(nc.semaphore("s_" + snm))

        per = ExitStack()

        def sb(st, name, shape, dt):
            return st.enter_context(nc.sbuf_tensor(name, list(shape), dt))

        s_chvec = sb(per, "s_chvec", (128, 7 * CD), F32)
        s_acols = sb(per, "s_acols", (128, CD * DS), F32)
        s_normw = sb(per, "s_normw", (128, DM), F32)
        s_chain = sb(per, "s_chain", (128, 16), F32)
        s_id32 = sb(per, "s_id32", (128, 128), F32)
        s_idbf = sb(per, "s_idbf", (128, 128), BF16)
        s_ones32 = sb(per, "s_ones32", (1, 128), F32)
        s_onesbf = sb(per, "s_onesbf", (1, 128), BF16)
        s_selbf = sb(per, "s_selbf", (32, 32 * 128), BF16)
        s_wxp = sb(per, "s_wxp", (128, CD * 96), BF16)
        s_wdt = sb(per, "s_wdt", (64, DI), BF16)
        s_scol = sb(per, "s_scol", (128, CD * DS), F32)
        s_dssum = sb(per, "s_dssum", (128, CD), F32)
        s_xmax = sb(per, "s_xmax", (128, 1), F32)
        s_stat = sb(per, "s_stat", (128, 16), F32)
        s_bc0 = sb(per, "s_bc0", (32, T), BF16)
        s_onet = sb(per, "s_onet", (128, T), F32)
        s_scl = sb(per, "s_scl", (128, 24), F32)
        s_sclr = sb(per, "s_sclr", (4, 16), F32)
        s_m1 = sb(per, "s_m1", (128, 4), F32)
        s_kc = sb(per, "s_kc", (128, 4), F32)
        nc.const_aps.register(F32, EPS, s_kc[:, 0:1])
        nc.const_aps.register(F32, 32768.5, s_kc[:, 1:2])
        nc.const_aps.register(F32, 0.5, s_kc[:, 2:3])
        nc.const_aps.register(F32, 8.5, s_kc[:, 3:4])

        p_mm = [stack.enter_context(nc.psum_tensor("p_mm%d" % i, [128, 512], F32))
                for i in range(4)]
        p_tr = [stack.enter_context(nc.psum_tensor("p_tr%d" % i, [128, 256], BF16))
                for i in range(2)]
        p_tf = [stack.enter_context(nc.psum_tensor("p_tf%d" % i, [128, 128], F32))
                for i in range(2)]

        pro = ExitStack()
        WSL = 3
        s_wload = sb(pro, "s_wload", (128, WSL * 4096), F32)
        s_wbf = sb(pro, "s_wbf", (128, WSL * 4096), BF16)
        s_wi32 = sb(pro, "s_wi32", (128, WSL * 3072), I32)
        s_qs = sb(pro, "s_qs", (128, WSL * 96), F32)
        s_qsi = sb(pro, "s_qsi", (128, WSL * 96), F32)
        s_alog = sb(pro, "s_alog", (128, CD * DS), F32)

        O("sync", lambda e: e.dma_start(out=s_chvec[:], in_=chvec),
          writes=["chvec"], dma="d_c")
        O("sync", lambda e: e.dma_start(out=s_alog[:], in_=alog),
          writes=["alog"], dma="d_c")
        O("sync", lambda e: e.dma_start(out=s_normw[:], in_=normw),
          writes=["normw"], dma="d_c")
        O("sync", lambda e: e.dma_start(out=s_chain[:], in_=chain_ab),
          writes=["chain"], dma="d_c")
        O("sync", lambda e: e.dma_start(out=s_id32[:], in_=ident),
          writes=["id32"], dma="d_c")
        O("sync", lambda e: e.dma_start(out=s_ones32[:], in_=ones_r),
          writes=["ones32"], dma="d_c")
        O("scalar", lambda e: e.copy(
            s_selbf[:].rearrange("p (n m) -> p n m", m=128),
            s_id32[0:32, 0:32].rearrange(
                "p n -> p n ()").broadcast_to([32, 32, 128])),
          reads=["id32"], writes=["selbf"])
        O("scalar", lambda e: e.activation(s_acols[:], s_alog[:], AT.Exp),
          reads=["alog"], writes=["acols"])
        O("vector", lambda e: e.tensor_scalar_mul(s_acols[:], s_acols[:], -1.0),
          reads=["acols"], writes=["acols"])
        O("scalar", lambda e: e.copy(s_idbf[:], s_id32[:]),
          reads=["id32"], writes=["idbf"])
        O("scalar", lambda e: e.copy(s_onesbf[:], s_ones32[:]),
          reads=["ones32"], writes=["onesbf"])
        O("vector", lambda e: e.memset(s_onet[:], 1.0), writes=["onet"])
        O("vector", lambda e: e.memset(s_scol[:], 0.0), writes=["scolA"])
        O("vector", lambda e: e.memset(s_dssum[:], 0.0), writes=["dssum"])
        O("vector", lambda e: e.memset(s_xmax[:], 0.0), writes=["xmax"])
        O("vector", lambda e: e.memset(s_sclr[:], 0.0), writes=["sclr"])
        O("vector", lambda e: e.memset(s_kc[:, 0:1], EPS), writes=["kconst"])
        O("vector", lambda e: e.memset(s_kc[:, 1:2], 32768.5),
          reads=["kconst"], writes=["kconst"])
        O("vector", lambda e: e.memset(s_kc[:, 2:3], 0.5),
          reads=["kconst"], writes=["kconst"])
        O("vector", lambda e: e.memset(s_kc[:, 3:4], 8.5),
          reads=["kconst"], writes=["kconst"])
        O("scalar", lambda e: e.dma_start(out=ccm_in[:, :], in_=s_sclr[0:1, 0:8]),
          reads=["sclr"], writes=["ccmin_h"], dma="d_sp")

        for k in range(8):
            sl = k % WSL
            key = "wld%d" % sl
            wls = s_wload[:, 4096 * sl:4096 * (sl + 1)]
            wbs = s_wbf[:, 4096 * sl:4096 * (sl + 1)]
            for h, pool in ((0, "d_w"), (1, "d_r")):
                O("sync", lambda e, k=k, wls=wls, h=h: e.dma_start(
                    out=wls[:, 2048 * h:2048 * (h + 1)],
                    in_=w_in[128 * k:128 * (k + 1),
                             2048 * h:2048 * (h + 1)]),
                  reads=[key + "h%d" % h], writes=[key + "h%d" % h], dma=pool)
            O("gpsimd", lambda e, wls=wls, wbs=wbs: e.tensor_copy(wbs, wls),
              reads=[key + "h0", key + "h1", "wbf%d" % sl],
              writes=["wbf%d" % sl])
            for mg in range(16):
                O("scalar", lambda e, k=k, mg=mg, wbs=wbs: e.dma_start(
                    out=h_win[mg, k], in_=wbs[:, 256 * mg:256 * (mg + 1)]),
                  reads=["wbf%d" % sl], writes=["h_win"], dma="d_sp")
        for k in range(16):
            sl = k % WSL
            key = "wld%d" % sl
            wls = s_wload[:, 4096 * sl:4096 * sl + DM]
            wbs = s_wbf[:, 4096 * sl:4096 * sl + DM]
            for h, pool in ((0, "d_w"), (1, "d_r")):
                O("sync", lambda e, k=k, wls=wls, h=h: e.dma_start(
                    out=wls[:, 512 * h:512 * (h + 1)],
                    in_=w_out[128 * k:128 * (k + 1),
                              512 * h:512 * (h + 1)]),
                  reads=[key + "h%d" % h], writes=[key + "h%d" % h], dma=pool)
            O("gpsimd", lambda e, wls=wls, wbs=wbs: e.tensor_copy(wbs, wls),
              reads=[key + "h0", key + "h1", "wbf%d" % sl],
              writes=["wbf%d" % sl])
            for mg in range(4):
                O("scalar", lambda e, k=k, mg=mg, wbs=wbs: e.dma_start(
                    out=h_wout[mg, k], in_=wbs[:, 256 * mg:256 * (mg + 1)]),
                  reads=["wbf%d" % sl], writes=["h_wout"], dma="d_sp")
        for k in range(CD):
            sl = k % WSL
            key = "wld%d" % sl
            wls = s_wload[:, 4096 * sl:4096 * sl + 96]
            O("sync", lambda e, k=k, wls=wls: e.dma_start(
                out=wls, in_=w_xp[128 * k:128 * (k + 1), :]),
              reads=[key + "h0"], writes=[key + "h0"], dma="d_w")
            O("gpsimd", lambda e, k=k, wls=wls: e.tensor_copy(
                s_wxp[:, 96 * k:96 * (k + 1)], wls),
              reads=[key + "h0"], writes=["wxp"])
        O("sync", lambda e: e.dma_start(out=s_wload[0:64, 0:DI], in_=w_dt),
          reads=["wld0h0", "wld0h1"], writes=["wld0h0", "wld0h1"], dma="d_w")
        O("scalar", lambda e: e.copy(s_wdt[:], s_wload[0:64, 0:DI]),
          reads=["wld0h0", "wld0h1"], writes=["wdt"])

        def ffn_prep(w_ap, h_dst, hkey, nrows, ncols):
            nb32 = ncols // 32
            for k in range(nrows // 128):
                sl = k % WSL
                key = "wld%d" % sl
                kq, kqi, ki32, kbf = ("qs%d" % sl, "qsi%d" % sl,
                                      "wi32%d" % sl, "wbf%d" % sl)
                wls = s_wload[:, 4096 * sl:4096 * sl + ncols]
                wbs = s_wbf[:, 4096 * sl:4096 * sl + ncols]
                wis = s_wi32[:, 3072 * sl:3072 * sl + ncols]
                qss = s_qs[:, 96 * sl:96 * sl + nb32]
                qis = s_qsi[:, 96 * sl:96 * sl + nb32]
                hc = ncols // 2
                for h, pool in ((0, "d_w"), (1, "d_r")):
                    O("sync", lambda e, k=k, wls=wls, h=h, hc=hc: e.dma_start(
                        out=wls[:, hc * h:hc * (h + 1)],
                        in_=w_ap[128 * k:128 * (k + 1), hc * h:hc * (h + 1)]),
                      reads=[key + "h%d" % h], writes=[key + "h%d" % h],
                      dma=pool)
                rv = wls.rearrange("p (b e) -> p b e", e=32)
                O("vector", lambda e, rv=rv, qss=qss: e.tensor_reduce(
                    qss, rv, axis=AX, op=OP.max,
                    apply_absolute_value=True),
                  reads=[key + "h0", key + "h1", kq], writes=[kq])
                O("vector", lambda e, qss=qss: e.tensor_scalar(
                    qss, qss,
                    float(np.float32(1.0) / np.float32(7.0)), 1e-12,
                    OP.mult, OP.max),
                  reads=[kq], writes=[kq])
                O("vector", lambda e, qss=qss, qis=qis: e.reciprocal(qis, qss),
                  reads=[kq, kqi], writes=[kqi])

                def bcq(t):
                    return t.rearrange("p b -> p b ()",
                                       b=nb32).broadcast_to([128, nb32, 32])
                O("vector", lambda e, rv=rv, qis=qis: e.tensor_tensor(
                    rv, rv, bcq(qis), OP.mult),
                  reads=[key + "h0", key + "h1", kqi],
                  writes=[key + "h0", key + "h1"])
                O("scalar", lambda e, wis=wis, wls=wls: e.activation(
                    wis, wls, AT.Identity),
                  reads=[key + "h0", key + "h1", ki32], writes=[ki32])
                O("vector", lambda e, wbs=wbs, wis=wis, qss=qss:
                  e.tensor_tensor(
                    wbs.rearrange("p (b e) -> p b e", e=32),
                    wis.rearrange("p (b e) -> p b e", e=32),
                    bcq(qss), OP.mult),
                  reads=[ki32, kq, kbf], writes=[kbf])
                O("scalar", lambda e, k=k, wbs=wbs: e.dma_start(
                    out=h_dst[128 * k:128 * (k + 1), :], in_=wbs),
                  reads=[kbf], writes=[hkey], dma="d_sp")

        ffn_prep(w_u, h_wu, "h_wu", 3072, DM)
        ffn_prep(w_g, h_wg, "h_wg", 3072, DM)
        ffn_prep(w_d, h_wd, "h_wd", DM, 3072)
        pro.close()
        P.barrier()
        if stop == "pro":
            _end()
            return nc

        mam = ExitStack()
        s_win = sb(mam, "s_win", (128, 2 * 2048), BF16)
        s_wo = sb(mam, "s_wo", (128, 2 * 4096), BF16)
        s_x = sb(mam, "s_x", (128, 2 * TT * DM), F32)
        s_xnbf = sb(mam, "s_xnbf", (128, TT * DM), BF16)
        s_xnfm = sb(mam, "s_xnfm", (128, CM * T), BF16)
        s_xnfm3 = sb(mam, "s_xnfm3", (128, 3 * CM), BF16)
        s_xiw = sb(mam, "s_xiw", (128, 2 * 262), F32)
        s_halo = sb(mam, "s_halo", (128, CD * 3), F32)
        s_u = sb(mam, "s_u", (128, CD * T), F32)
        s_ubf = sb(mam, "s_ubf", (128, CD * T), BF16)
        s_sres = sb(mam, "s_sres", (128, CD * T), BF16)
        s_drbf = sb(mam, "s_drbf", (64, T), BF16)
        s_bcrow = sb(mam, "s_bcrow", (32, T), BF16)
        s_delta = sb(mam, "s_delta", (128, CD * T), F32)
        s_brep = sb(mam, "s_brep", (128, DS * T), BF16)
        s_crep = sb(mam, "s_crep", (128, DS * T), BF16)
        s_sp = sb(mam, "s_sp", (128, DS * (T + 1)), F32)
        s_dAq = sb(mam, "s_dAq", (128, 2 * 4 * (T + 1)), F32)
        s_wdu = sb(mam, "s_wdu", (128, T), F32)
        s_ybuf = sb(mam, "s_ybuf", (128, T), F32)
        s_ygbf = sb(mam, "s_ygbf", (128, 2 * CD * T), BF16)
        s_attn = sb(mam, "s_attn", (128, CM * T), F32)
        s_x1 = sb(mam, "s_x1", (128, TT * DM), F32)
        s_e1 = sb(mam, "s_e1", (128, 2 * T), F32)
        s_xb = sb(mam, "s_xb", (128, 2 * T), F32)
        s_gath = sb(mam, "s_gath", (128, 2 * CD * DS), F32)
        s_sin = sb(mam, "s_sin", (128, CD * DS), F32)
        s_tmp = sb(mam, "s_tmp", (128, CD * DS), F32)
        s_cums = sb(mam, "s_cums", (128, T), F32)

        import sys
        print("SBUF remaining after mamba allocs:", nc.sbuf_bytes_remaining,
              file=sys.stderr)
        O("vector", lambda e: e.memset(s_dAq[:], 0.0),
          writes=["dAq0", "dAq1"])
        O("sync", lambda e: e.dma_start(out=s_x[0:3, 0:DM], in_=xh3),
          writes=["xst0"], dma="d_x")
        O("scalar", lambda e: e.activation(
            s_x1[0:3, 0:DM], s_x[0:3, 0:DM], AT.Square,
            accum_out=s_stat[0:3, 0:1]),
          reads=["xst0", "x1t0"], writes=["hst", "x1t0"])
        O("scalar", lambda e: e.activation(
            s_stat[0:3, 1:2], s_stat[0:3, 0:1], AT.Sqrt,
            scale=1.0 / DM, bias=EPS),
          reads=["hst"], writes=["hst1"])
        O("vector", lambda e: e.reciprocal(s_stat[0:3, 2:3], s_stat[0:3, 1:2]),
          reads=["hst1"], writes=["hst2"])
        O("vector", lambda e: e.scalar_tensor_tensor(
            s_xnbf[0:3, 0:DM], s_x[0:3, 0:DM], s_stat[0:3, 2:3],
            s_normw[0:3, :], OP.mult, OP.mult),
          reads=["xst0", "hst2", "normw"], writes=["hxn"])
        for kc in range(CM):
            pk = "ptr%d" % (kc % 2)
            O("tensor", lambda e, kc=kc: e.transpose(
                p_tr[kc % 2][:, 0:3], s_xnbf[0:3, 128 * kc:128 * (kc + 1)],
                s_idbf[0:3, 0:3]),
              reads=["hxn", "idbf"], writes=[pk])
            O("scalar", lambda e, kc=kc, pk=pk: e.copy(
                s_xnfm3[:, 3 * kc:3 * kc + 3], p_tr[kc % 2][:, 0:3]),
              reads=[pk], writes=["hxnf%d" % kc])
        for mg in range(8):
            O("sync", lambda e, mg=mg: e.dma_start(
                out=s_win[:, 0:2048].rearrange("p (k j) -> p k j", j=256),
                in_=h_win[mg].rearrange("k p j -> p k j")),
              reads=["h_win"], writes=["win0h0", "win0h1"], dma="d_w")
            for sub in range(2):
                c = 2 * mg + sub
                fr = []
                for k in range(8):
                    fr.append((
                        (lambda e, k=k, sub=sub: e.matmul(
                            p_mm[2][:, 0:3],
                            s_win[:, 256 * k + 128 * sub:
                                  256 * k + 128 * sub + 128],
                            s_xnfm3[:, 3 * k:3 * k + 3],
                            start=(k == 0), stop=(k == 7))),
                        ["win0h0", "win0h1", "hxnf%d" % k]))
                P.mm_group(fr, writes=["pmm2"])
                O("scalar", lambda e, c=c: e.copy(
                    s_halo[:, 3 * c:3 * c + 3], p_mm[2][:, 0:3]),
                  reads=["pmm2"], writes=["halo%d" % c])

        def head_pre(b):
            xslot = b % 2
            t0 = b * T
            for i in range(TT):
                O("sync", lambda e, i=i, t0=t0, xslot=xslot: e.dma_start(
                    out=s_x[:, 2048 * xslot + DM * i:2048 * xslot + DM * (i + 1)],
                    in_=x_in[t0 + 128 * i:t0 + 128 * (i + 1), :]),
                  writes=["xst%d" % xslot], dma="d_x")
            for i in range(TT):
                xt = s_x[:, 2048 * xslot + DM * i:2048 * xslot + DM * (i + 1)]
                O("scalar", lambda e, xt=xt: e.activation(
                    s_x1[:, 0:DM], xt, AT.Square, accum_out=s_stat[:, 0:1]),
                  reads=["xst%d" % xslot, "x1t0"], writes=["st0", "x1t0"])
                O("scalar", lambda e: e.activation(
                    s_stat[:, 1:2], s_stat[:, 0:1], AT.Sqrt,
                    scale=1.0 / DM, bias=EPS),
                  reads=["st0", "kconst"], writes=["st1"])
                O("vector", lambda e: e.reciprocal(
                    s_stat[:, 2:3], s_stat[:, 1:2]),
                  reads=["st1"], writes=["st2"])
                O("vector", lambda e, xt=xt, i=i: e.scalar_tensor_tensor(
                    s_xnbf[:, DM * i:DM * (i + 1)], xt, s_stat[:, 2:3],
                    s_normw[:], OP.mult, OP.mult),
                  reads=["xst%d" % xslot, "st2", "normw"],
                  writes=["xnbf%d" % i])
                for kc in range(CM):
                    pk = "ptr%d" % (kc % 2)
                    O("tensor", lambda e, i=i, kc=kc: e.transpose(
                        p_tr[kc % 2][:, 0:128],
                        s_xnbf[:, DM * i + 128 * kc:DM * i + 128 * (kc + 1)],
                        s_idbf[:]),
                      reads=["xnbf%d" % i, "idbf"], writes=[pk])
                    O("scalar", lambda e, i=i, kc=kc, pk=pk: e.copy(
                        s_xnfm[:, T * kc + 128 * i:T * kc + 128 * (i + 1)],
                        p_tr[kc % 2][:, 0:128]),
                      reads=[pk], writes=["xnfm%d" % kc])
        def head_mg_dma(b, mg, slot):
            ws = "win%d" % slot
            for h, pool in ((0, "d_w"), (1, "d_r")):
                O("sync", lambda e, mg=mg, slot=slot, h=h: e.dma_start(
                    out=s_win[:, slot * 2048 + 1024 * h:
                              slot * 2048 + 1024 * (h + 1)]
                    .rearrange("p (k j) -> p k j", j=256),
                    in_=h_win[mg, 4 * h:4 * (h + 1)].rearrange(
                        "k p j -> p k j")),
                  reads=["h_win"], writes=[ws + "h%d" % h], dma=pool)

        def head_mg(b, mg, slot=None, dma=True):
            if True:
                if slot is None:
                    slot = mg % 2
                ws = "win%d" % slot
                if dma:
                    head_mg_dma(b, mg, slot)
                for sub in range(2):
                    c = 2 * mg + sub
                    pm = "pmm%d" % (c % 2)
                    fr = []
                    for k in range(8):
                        fr.append((
                            (lambda e, k=k, sub=sub, mg=mg, c=c, slot=slot:
                             e.matmul(
                                p_mm[c % 2][:, 0:T],
                                s_win[:, slot * 2048 + 256 * (k % 4)
                                      + 1024 * (k // 4) + 128 * sub:
                                      slot * 2048 + 256 * (k % 4)
                                      + 1024 * (k // 4) + 128 * sub + 128],
                                s_xnfm[:, T * k:T * (k + 1)],
                                start=(k == 0), stop=(k == 7))),
                            [ws + "h%d" % (k // 4), "xnfm%d" % k]))
                    P.mm_group(fr, writes=[pm])
                    if c < CD:
                        xw = "xiw%d" % (c % 2)
                        base = (c % 2) * 262
                        O("scalar", lambda e, c=c, base=base: e.copy(
                            s_xiw[:, base + 3:base + 3 + T],
                            p_mm[c % 2][:, 0:T]),
                          reads=[pm], writes=[xw])
                        O("scalar", lambda e, c=c, base=base: e.copy(
                            s_xiw[:, base:base + 3], s_halo[:, 3 * c:3 * c + 3]),
                          reads=["halo%d" % c], writes=[xw + "h"])
                        O("vector", lambda e, c=c, base=base: e.tensor_scalar(
                            s_u[:, T * c:T * (c + 1)],
                            s_xiw[:, base:base + T],
                            s_chvec[:, 7 * c:7 * c + 1], None, OP.mult),
                          reads=[xw, xw + "h", "chvec"], writes=["uc%d" % c])
                        for j in range(1, 4):
                            O("vector", lambda e, c=c, base=base, j=j:
                              e.scalar_tensor_tensor(
                                  s_u[:, T * c:T * (c + 1)],
                                  s_xiw[:, base + j:base + j + T],
                                  s_chvec[:, 7 * c + j:7 * c + j + 1],
                                  s_u[:, T * c:T * (c + 1)],
                                  OP.mult, OP.add),
                              reads=[xw, xw + "h", "chvec", "uc%d" % c],
                              writes=["uc%d" % c])
                        O("gpsimd", lambda e, c=c, base=base: e.tensor_copy(
                            s_halo[:, 3 * c:3 * c + 3],
                            s_xiw[:, base + T:base + T + 3]),
                          reads=[xw], writes=["halo%d" % c])
                        O("scalar", lambda e, c=c: e.activation(
                            s_u[:, T * c:T * (c + 1)],
                            s_u[:, T * c:T * (c + 1)], AT.Silu,
                            bias=s_chvec[:, 7 * c + 4:7 * c + 5]),
                          reads=["uc%d" % c, "chvec"], writes=["uc%d" % c])
                        O("gpsimd", lambda e, c=c: e.tensor_copy(
                            s_ubf[:, T * c:T * (c + 1)],
                            s_u[:, T * c:T * (c + 1)]),
                          reads=["uc%d" % c], writes=["ubf%d" % c])
                        if XPROJ_IN_MG:
                            O("tensor", lambda e, c=c: e.matmul(
                                p_mm[2][0:96, 0:T],
                                s_wxp[:, 96 * c:96 * (c + 1)],
                                s_ubf[:, T * c:T * (c + 1)],
                                start=(c == 0), stop=(c == CD - 1)),
                              reads=["wxp", "ubf%d" % c], writes=["pmm2"])
                        if b == 0 and DBG_HBM:
                            O("scalar", lambda e, c=c: e.dma_start(
                                out=h_dbg_u[128 * c:128 * (c + 1), :],
                                in_=s_u[:, T * c:T * (c + 1)]),
                              reads=["uc%d" % c], dma="d_sp")
                            O("scalar", lambda e, c=c, base=base: e.dma_start(
                                out=h_dbg_xi[128 * c:128 * (c + 1), :],
                                in_=s_xiw[:, base + 3:base + 3 + T]),
                              reads=[xw], dma="d_sp")
                    else:
                        c2 = c - CD
                        O("scalar", lambda e, c2=c2, c=c: e.activation(
                            s_sres[:, T * c2:T * (c2 + 1)],
                            p_mm[c % 2][:, 0:T], AT.Silu),
                          reads=[pm], writes=["sres%d" % c2])
        def head_post(b):
            if not XPROJ_IN_MG:
                fr = []
                for k in range(CD):
                    fr.append((
                        (lambda e, k=k: e.matmul(
                            p_mm[2][0:96, 0:T], s_wxp[:, 96 * k:96 * (k + 1)],
                            s_ubf[:, T * k:T * (k + 1)],
                            start=(k == 0), stop=(k == CD - 1))),
                        ["wxp", "ubf%d" % k]))
                P.mm_group(fr, writes=["pmm2"])
            if b == 0 and DBG_HBM:
                O("scalar", lambda e: e.copy(s_xb[0:96, 0:T], p_mm[2][0:96, 0:T]),
                  reads=["pmm2", "xb0"], writes=["xb0"])
                O("scalar", lambda e: e.dma_start(
                    out=h_dbg_xdbl[:, :], in_=s_xb[0:96, 0:T]),
                  reads=["xb0"], dma="d_sp")
            O("scalar", lambda e: e.copy(s_drbf[:, 0:T], p_mm[2][0:64, 0:T]),
              reads=["pmm2"], writes=["drbf"])
            O("scalar", lambda e: e.copy(s_bcrow[:, 0:T], p_mm[2][64:96, 0:T]),
              reads=["pmm2"], writes=["bcrow"])
            if b == 0:
                O("scalar", lambda e: e.copy(s_bc0[:, 0:T], s_bcrow[:, 0:T]),
                  reads=["bcrow"], writes=["bc0"])
            for n in range(DS):
                pb = "pmm%d" % (2 + (n % 2))
                O("tensor", lambda e, n=n: e.matmul(
                    p_mm[2 + (n % 2)][:, 0:T],
                    s_selbf[:, 128 * n:128 * (n + 1)],
                    s_bcrow[0:32, 0:T], start=True, stop=True),
                  reads=["bcrow", "selbf"], writes=[pb])
                O("scalar", lambda e, n=n, pb=pb: e.copy(
                    s_brep[:, T * n:T * (n + 1)], p_mm[2 + (n % 2)][:, 0:T]),
                  reads=[pb, "brepA"], writes=["brepA"])
                O("tensor", lambda e, n=n: e.matmul(
                    p_mm[2 + (n % 2)][:, 0:T],
                    s_selbf[:, 128 * (16 + n):128 * (17 + n)],
                    s_bcrow[0:32, 0:T], start=True, stop=True),
                  reads=["bcrow", "selbf"], writes=[pb])
                O("scalar", lambda e, n=n, pb=pb: e.copy(
                    s_crep[:, T * n:T * (n + 1)], p_mm[2 + (n % 2)][:, 0:T]),
                  reads=[pb, "crepA"], writes=["crepA"])
            def dt_finish(c):
                xo = (c % 2) * T
                O("vector", lambda e, c=c, xo=xo: e.scalar_tensor_tensor(
                    s_delta[:, T * c:T * (c + 1)], s_xb[:, xo:xo + T], 0.0,
                    s_e1[:, xo:xo + T], OP.max, OP.add),
                  reads=["xb%d" % (c % 2), "e1%d" % (c % 2)],
                  writes=["dl%d" % c])
                O("vector", lambda e, c=c: e.reduce_sum(
                    s_stat[:, 8:9], s_delta[:, T * c:T * (c + 1)], axis=AX),
                  reads=["dl%d" % c], writes=["dred"])
                O("vector", lambda e, c=c: e.tensor_add(
                    s_dssum[:, c:c + 1], s_dssum[:, c:c + 1], s_stat[:, 8:9]),
                  reads=["dred", "dssum"], writes=["dssum"])
            for c in range(CD):
                pm = "pmm%d" % (c % 2)
                xo = (c % 2) * T
                O("tensor", lambda e, c=c: e.matmul(
                    p_mm[c % 2][:, 0:T], s_wdt[:, 128 * c:128 * (c + 1)],
                    s_drbf[:, 0:T], start=True, stop=True),
                  reads=["wdt", "drbf"], writes=[pm])
                O("scalar", lambda e, c=c, pm=pm, xo=xo: e.activation(
                    s_xb[:, xo:xo + T], p_mm[c % 2][:, 0:T], AT.Identity,
                    bias=s_chvec[:, 7 * c + 6:7 * c + 7]),
                  reads=[pm, "chvec"], writes=["xb%d" % (c % 2)])
                O("vector", lambda e, xo=xo, c=c: e.scalar_tensor_tensor(
                    s_e1[:, xo:xo + T], s_xb[:, xo:xo + T], -1.0,
                    s_xb[:, xo:xo + T], OP.mult, OP.min),
                  reads=["xb%d" % (c % 2)], writes=["e1%d" % (c % 2)])
                O("scalar", lambda e, xo=xo, c=c: e.activation(
                    s_e1[:, xo:xo + T], s_e1[:, xo:xo + T], AT.Exp),
                  reads=["e1%d" % (c % 2)], writes=["e1%d" % (c % 2)])
                O("scalar", lambda e, xo=xo, c=c: e.activation(
                    s_e1[:, xo:xo + T], s_e1[:, xo:xo + T], AT.Ln, bias=1.0),
                  reads=["e1%d" % (c % 2)], writes=["e1%d" % (c % 2)])
                if c >= 1:
                    dt_finish(c - 1)
            dt_finish(CD - 1)

        def block_scan_chunk(b, c):
            TP = T + 1
            O("vector", lambda e, c=c: e.tensor_mul(
                s_wdu[:, 0:T], s_delta[:, T * c:T * (c + 1)],
                s_u[:, T * c:T * (c + 1)]),
              reads=["dl%d" % c, "uc%d" % c], writes=["wdu"])

            spv = s_sp[:].rearrange("p (n t) -> p n t", n=DS)
            O("vector", lambda e, spv=spv: e.tensor_tensor(
                spv[:, :, 1:],
                s_wdu[:, 0:T].rearrange("p t -> p () t").broadcast_to(
                    [128, DS, T]),
                s_brep[:].rearrange("p (n t) -> p n t", n=DS), OP.mult),
              reads=["wdu", "brepA", "spall"], writes=["spall"])
            O("vector", lambda e, c=c, spv=spv: e.tensor_copy(
                spv[:, :, 0:1], s_scol[:, DS * c:DS * (c + 1)].rearrange(
                    "p n -> p n ()")),
              reads=["scolA", "spall"], writes=["spall"])
            QP = 4 * (T + 1)
            for g in range(4):
                dk = "dAq%d" % (g % 2)
                for n in range(4 * g, 4 * g + 4):
                    O("scalar", lambda e, c=c, n=n, g=g: e.activation(
                        s_dAq[:, (g % 2) * QP + TP * (n - 4 * g) + 1:
                              (g % 2) * QP + TP * (n - 4 * g + 1)],
                        s_delta[:, T * c:T * (c + 1)], AT.Exp,
                        scale=s_acols[:, DS * c + n:DS * c + n + 1]),
                      reads=["dl%d" % c, "acols", dk], writes=[dk])
                O("vector", lambda e, g=g: e.tensor_tensor_scan(
                    s_sp[:, QP * g:QP * (g + 1)],
                    s_dAq[:, (g % 2) * QP:(g % 2 + 1) * QP],
                    s_sp[:, QP * g:QP * (g + 1)], 0.0,
                    OP.mult, OP.add),
                  reads=[dk, "spall"], writes=["spall"])
            O("vector", lambda e, c=c: e.tensor_copy(
                s_scol[:, DS * c:DS * (c + 1)],
                s_sp[:].rearrange("p (n t) -> p n t", n=DS)[:, :, T:T + 1]),
              reads=["spall", "scolA"], writes=["scolA"])
            O("vector", lambda e, spv=spv: e.tensor_tensor(
                spv[:, :, 1:], spv[:, :, 1:],
                s_crep[:].rearrange("p (n t) -> p n t", n=DS), OP.mult),
              reads=["spall", "crepA"], writes=["spall"])
            TP = T + 1
            for step in (1, 2, 4, 8):
                O("vector", lambda e, step=step, spv=spv:
                  e.tensor_tensor(
                      spv[:, 0:DS:2 * step, 1:],
                      spv[:, 0:DS:2 * step, 1:],
                      spv[:, step:DS:2 * step, 1:],
                      OP.add),
                  reads=["spall"], writes=["spall"])
            O("vector", lambda e, c=c: e.scalar_tensor_tensor(
                s_ybuf[:, 0:T], s_u[:, T * c:T * (c + 1)],
                s_chvec[:, 7 * c + 5:7 * c + 6], s_sp[:, 1:T + 1],
                OP.mult, OP.add),
              reads=["uc%d" % c, "chvec", "spall", "ybuf"], writes=["ybuf"])

        def block_gate_chunk(b, c):
            ys = (b % 2) * CD * T
            O("vector", lambda e, c=c, ys=ys: e.tensor_mul(
                s_ygbf[:, ys + T * c:ys + T * (c + 1)], s_ybuf[:, 0:T],
                s_sres[:, T * c:T * (c + 1)]),
              reads=["ybuf", "sres%d" % c], writes=["yg%d.%d" % (b % 2, c)])

        def tail_mm_dma(b, mg):
            ws = "wo%d" % (mg % 2)
            for h, pool in ((0, "d_w"), (1, "d_r")):
                O("sync", lambda e, mg=mg, h=h: e.dma_start(
                    out=s_wo[:, (mg % 2) * 4096 + 2048 * h:
                             (mg % 2) * 4096 + 2048 * (h + 1)]
                    .rearrange("p (k j) -> p k j", j=256),
                    in_=h_wout[mg, 8 * h:8 * (h + 1)].rearrange(
                        "k p j -> p k j")),
                  reads=["h_wout"], writes=[ws + "h%d" % h], dma=pool)

        def tail_mm(b, xslot, mg, dma=True):
            ys = (b % 2) * CD * T
            if True:
                ws = "wo%d" % (mg % 2)
                if dma:
                    tail_mm_dma(b, mg)
                for sub in range(2):
                    m = 2 * mg + sub
                    pm = "pmm%d" % (m % 2)
                    fr = []
                    for k in range(CD):
                        fr.append((
                            (lambda e, k=k, mg=mg, sub=sub, m=m, ys=ys:
                             e.matmul(
                                 p_mm[m % 2][:, 0:T],
                                 s_wo[:, (mg % 2) * 4096 + 256 * (k % 8)
                                       + 2048 * (k // 8) + 128 * sub:
                                       (mg % 2) * 4096 + 256 * (k % 8)
                                       + 2048 * (k // 8) + 128 * sub + 128],
                                 s_ygbf[:, ys + T * k:ys + T * (k + 1)],
                                 start=(k == 0), stop=(k == CD - 1))),
                            [ws + "h%d" % (k // 8), "yg%d.%d" % (b % 2, k)]))
                    P.mm_group(fr, writes=[pm])
                    O("scalar", lambda e, m=m, pm=pm: e.copy(
                        s_attn[:, T * m:T * (m + 1)], p_mm[m % 2][:, 0:T]),
                      reads=[pm, "lnsq"], writes=["attn%d" % m])
        def tail_fin(b, xslot):
            t0 = b * T
            for m in range(CM):
                for i in range(TT):
                    pf = "ptf%d" % ((m * TT + i) % 2)
                    O("tensor", lambda e, m=m, i=i: e.transpose(
                        p_tf[(m * TT + i) % 2][:],
                        s_attn[:, T * m + 128 * i:T * m + 128 * (i + 1)],
                        s_id32[:]),
                      reads=["attn%d" % m, "id32"], writes=[pf])
                    O("vector", lambda e, m=m, i=i, xslot=xslot, pf=pf:
                      e.tensor_add(
                          s_x1[:, DM * i + 128 * m:DM * i + 128 * (m + 1)],
                          p_tf[(m * TT + i) % 2][:],
                          s_x[:, 2048 * xslot + DM * i + 128 * m:
                              2048 * xslot + DM * i + 128 * (m + 1)]),
                      reads=[pf, "xst%d" % xslot], writes=["x1t%d" % i])
            for i in range(TT):
                x1t = s_x1[:, DM * i:DM * (i + 1)]
                O("scalar", lambda e, x1t=x1t: e.activation(
                    s_attn[:, 0:DM], x1t, AT.Square, accum_out=s_stat[:, 3:4]),
                  reads=["x1t%d" % i, "attn0", "attn1", "attn2", "attn3",
                         "attn4", "attn5", "attn6", "attn7"],
                  writes=["lst0", "lnsq"])
                O("vector", lambda e, x1t=x1t: e.reduce_sum(
                    s_stat[:, 4:5], x1t, axis=AX),
                  reads=["x1t%d" % i], writes=["lst1"])
                O("vector", lambda e: e.tensor_scalar_mul(
                    s_stat[:, 5:6], s_stat[:, 4:5], 1.0 / DM),
                  reads=["lst1"], writes=["lmean"])
                O("vector", lambda e: e.tensor_mul(
                    s_stat[:, 6:7], s_stat[:, 5:6], s_stat[:, 5:6]),
                  reads=["lmean"], writes=["lm2"])
                O("vector", lambda e: e.scalar_tensor_tensor(
                    s_stat[:, 7:8], s_stat[:, 3:4], 1.0 / DM,
                    s_stat[:, 6:7], OP.mult, OP.subtract),
                  reads=["lst0", "lm2"], writes=["lvar"])
                O("scalar", lambda e: e.activation(
                    s_stat[:, 9:10], s_stat[:, 7:8], AT.Sqrt, bias=EPS),
                  reads=["lvar", "kconst"], writes=["lsd"])
                O("vector", lambda e: e.reciprocal(
                    s_stat[:, 10:11], s_stat[:, 9:10]),
                  reads=["lsd"], writes=["lrs"])
                O("vector", lambda e: e.tensor_scalar_mul(
                    s_stat[:, 11:12], s_stat[:, 5:6], -1.0),
                  reads=["lmean"], writes=["lnm"])
                O("vector", lambda e, x1t=x1t, i=i: e.tensor_scalar(
                    x1t, x1t, s_stat[:, 11:12], s_stat[:, 10:11],
                    OP.add, OP.mult),
                  reads=["x1t%d" % i, "lnm", "lrs"], writes=["x1t%d" % i])
                O("vector", lambda e, x1t=x1t: e.tensor_reduce(
                    s_stat[:, 12:13], x1t, axis=AX, op=OP.max,
                    apply_absolute_value=True),
                  reads=["x1t%d" % i], writes=["lmx"])
                O("vector", lambda e: e.tensor_max(
                    s_xmax[:], s_xmax[:], s_stat[:, 12:13]),
                  reads=["lmx", "xmax"], writes=["xmax"])
                O("scalar", lambda e, i=i, t0=t0, x1t=x1t: e.dma_start(
                    out=h_x1n[t0 + 128 * i:t0 + 128 * (i + 1), :], in_=x1t),
                  reads=["x1t%d" % i], writes=["h_x1n"], dma="d_sp")

        def emit_chunk(b, c):
            block_scan_chunk(b, c)
            if b == 0:
                O("scalar", lambda e, c=c: e.dma_start(
                    out=h_ysave[128 * c:128 * (c + 1), :],
                    in_=s_ybuf[:, 0:T]),
                  reads=["ybuf"], writes=["h_ysave"], dma="d_sp")
                O("scalar", lambda e, c=c: e.dma_start(
                    out=h_dsave[128 * c:128 * (c + 1), :],
                    in_=s_delta[:, T * (c % 4) if False else T * c:
                        T * c + T]),
                  reads=["dl%d" % c], writes=["h_dsave"], dma="d_sp")
                O("scalar", lambda e, c=c: e.dma_start(
                    out=h_srsave[128 * c:128 * (c + 1), :],
                    in_=s_sres[:, T * c:T * (c + 1)]),
                  reads=["sres%d" % c], writes=["h_srsave"], dma="d_sp")
            else:
                block_gate_chunk(b, c)

        head_pre(0)
        for mg in range(16):
            head_mg(0, mg)
        head_post(0)
        HEAD_DMA_SCHED = {p - 1: lst for p, lst in HEAD_MG_SCHED.items()}
        for b in range(NB):
            nxt = b + 1 < NB
            for c in range(CD):
                emit_chunk(b, c)
                if WIN_PREFETCH and b >= 2 and c <= 2:
                    tail_mm_dma(b - 1, c + 1)
                if b >= 2 and c <= 3:
                    tail_mm(b - 1, (b - 1) % 2, c,
                            dma=not WIN_PREFETCH or c == 0 and False)
                if b >= 2 and c == 4:
                    tail_fin(b - 1, (b - 1) % 2)
                if nxt and INTERLEAVE_HEAD >= 1 and c == 5:
                    head_pre(b + 1)
                if nxt and INTERLEAVE_HEAD >= 2:
                    if c in HEAD_MG_SCHED:
                        for mg, slot in HEAD_MG_SCHED[c]:
                            head_mg(b + 1, mg, slot, dma=not WIN_PREFETCH)
                    if WIN_PREFETCH and c in HEAD_DMA_SCHED:
                        for mg, slot in HEAD_DMA_SCHED[c]:
                            head_mg_dma(b + 1, mg, slot)
                if nxt and INTERLEAVE_HEAD >= 3 and c == CD - 1:
                    head_post(b + 1)
                if WIN_PREFETCH and b >= 1 and c == CD - 1:
                    tail_mm_dma(b, 0)
            if nxt:
                if INTERLEAVE_HEAD < 1:
                    head_pre(b + 1)
                if INTERLEAVE_HEAD < 2:
                    for mg in range(16):
                        head_mg(b + 1, mg)
                if INTERLEAVE_HEAD < 3:
                    head_post(b + 1)
        for mg in range(4):
            tail_mm(NB - 1, (NB - 1) % 2, mg,
                    dma=(mg > 0 or not WIN_PREFETCH))
        tail_fin(NB - 1, (NB - 1) % 2)

        for c in range(CD):
            O("scalar", lambda e, c=c: e.activation(
                s_tmp[:, DS * c:DS * (c + 1)], s_acols[:, DS * c:DS * (c + 1)],
                AT.Exp, scale=s_dssum[:, c:c + 1]),
              reads=["acols", "dssum"], writes=["dtot"])
        O("scalar", lambda e: e.dma_start(
            out=cc_in[:, 0:CD * DS], in_=s_tmp[:]),
          reads=["dtot"], writes=["ccin_h"], dma="d_sp")
        O("scalar", lambda e: e.dma_start(
            out=cc_in[:, CD * DS:2 * CD * DS], in_=s_scol[:]),
          reads=["scolA"], writes=["ccin_h"], dma="d_sp")
        O("gpsimd", lambda e: e.collective_compute(
            "AllGather", OP.bypass, replica_groups=[list(range(cfg.NC))],
            ins=[cc_in.opt()], outs=[cc_out.opt()]),
          reads=["ccin_h"], writes=["ccout_h"])
        O("vector", lambda e: e.memset(s_sin[:], 0.0), writes=["sin"])
        for j in range(cfg.NC):
            gs = "gath0"
            off = 0
            O("sync", lambda e, j=j, off=off: e.dma_start(
                out=s_gath[:, off:off + 2 * CD * DS],
                in_=cc_out[128 * j:128 * (j + 1), :]),
              reads=["ccout_h"], writes=[gs], dma="d_x")
            O("vector", lambda e, j=j, off=off: e.tensor_scalar(
                s_tmp[:], s_gath[:, off:off + CD * DS],
                s_chain[:, 2 * j:2 * j + 1], s_chain[:, 2 * j + 1:2 * j + 2],
                OP.mult, OP.add),
              reads=[gs, "chain"], writes=["dtp"])
            O("vector", lambda e: e.tensor_mul(s_sin[:], s_sin[:], s_tmp[:]),
              reads=["dtp", "sin"], writes=["sin"])
            O("vector", lambda e, j=j, off=off: e.scalar_tensor_tensor(
                s_sin[:], s_gath[:, off + CD * DS:off + 2 * CD * DS],
                s_chain[:, 2 * j:2 * j + 1], s_sin[:], OP.mult, OP.add),
              reads=[gs, "chain", "sin"], writes=["sin"])
        for n in range(DS):
            pb = "pmm%d" % (2 + (n % 2))
            O("tensor", lambda e, n=n: e.matmul(
                p_mm[2 + (n % 2)][:, 0:T],
                s_selbf[:, 128 * (16 + n):128 * (17 + n)],
                s_bc0[0:32, 0:T], start=True, stop=True),
              reads=["bc0", "selbf"], writes=[pb])
            O("scalar", lambda e, n=n, pb=pb: e.copy(
                s_crep[:, T * n:T * (n + 1)], p_mm[2 + (n % 2)][:, 0:T]),
              reads=[pb, "crepA"], writes=["crepA"])
        for c in range(CD):
            O("scalar", lambda e, c=c: e.dma_start(
                out=s_e1[:, 0:T], in_=h_dsave[128 * c:128 * (c + 1), :]),
              reads=["h_dsave", "e10"], writes=["e10"], dma="d_r")
            O("scalar", lambda e, c=c: e.dma_start(
                out=s_ybuf[:, 0:T], in_=h_ysave[128 * c:128 * (c + 1), :]),
              reads=["h_ysave", "ybuf"], writes=["ybuf"], dma="d_r")
            O("scalar", lambda e, c=c: e.dma_start(
                out=s_sres[:, T * c:T * (c + 1)],
                in_=h_srsave[128 * c:128 * (c + 1), :]),
              reads=["h_srsave", "sres%d" % c], writes=["sres%d" % c],
              dma="d_r")
            O("vector", lambda e: e.tensor_tensor_scan(
                s_cums[:, 0:T], s_onet[:, 0:T], s_e1[:, 0:T], 0.0,
                OP.mult, OP.add),
              reads=["e10", "onet"], writes=["cums"])
            for n in range(DS):
                O("scalar", lambda e, c=c, n=n: e.activation(
                    s_sp[:, T * n:T * (n + 1)], s_cums[:, 0:T], AT.Exp,
                    scale=s_acols[:, DS * c + n:DS * c + n + 1]),
                  reads=["cums", "acols", "spall"], writes=["spall"])
                O("vector", lambda e, c=c, n=n: e.scalar_tensor_tensor(
                    s_sp[:, T * n:T * (n + 1)], s_sp[:, T * n:T * (n + 1)],
                    s_sin[:, DS * c + n:DS * c + n + 1],
                    s_crep[:, T * n:T * (n + 1)], OP.mult, OP.mult),
                  reads=["spall", "sin", "crepA"], writes=["spall"])
            spc = s_sp[:, 0:DS * T].rearrange("p (n t) -> p n t", n=DS)
            for step in (1, 2, 4, 8):
                O("vector", lambda e, step=step, spc=spc: e.tensor_tensor(
                    spc[:, 0:DS:2 * step, :], spc[:, 0:DS:2 * step, :],
                    spc[:, step:DS:2 * step, :], OP.add),
                  reads=["spall"], writes=["spall"])
            O("vector", lambda e: e.tensor_add(
                s_ybuf[:, 0:T], s_ybuf[:, 0:T], s_sp[:, 0:T]),
              reads=["spall", "ybuf"], writes=["ybuf"])
            block_gate_chunk(0, c)
        for i in range(TT):
            O("sync", lambda e, i=i: e.dma_start(
                out=s_x[:, DM * i:DM * (i + 1)],
                in_=x_in[128 * i:128 * (i + 1), :]),
              writes=["xst0"], dma="d_x")
        for mg in range(4):
            tail_mm(0, 0, mg)
        tail_fin(0, 0)

        mam.close()
        P.barrier()
        if stop == "mamba":
            _end()
            return nc

        ffn = ExitStack()
        s_xq = sb(ffn, "s_xq", (128, CM * LC), BF16)
        s_strip = sb(ffn, "s_strip", (128, 24576), BF16)
        s_f2 = sb(ffn, "s_f2", (128, 2 * 1024), F32)
        ffnA0 = ExitStack()
        s_f1 = sb(ffnA0, "s_f1", (128, 2 * 1024), F32)
        s_i32 = sb(ffnA0, "s_i32", (128, 2 * 1024), I32)
        s_fbf = sb(ffnA0, "s_fbf", (128, 2 * 1024), BF16)

        import sys
        print("SBUF remaining after ffn allocs:", nc.sbuf_bytes_remaining,
              file=sys.stderr)

        def colmax_to_scales(ncol_src, ccm_slot, scl_slot, nvals):
            O("tensor", lambda e, nvals=nvals: e.transpose(
                p_tf[0][0:nvals, :], s_m1[:, 0:nvals], s_id32[:]),
              reads=["m1", "id32"], writes=["ptf0"])
            O("vector", lambda e, nvals=nvals: e.reduce_max(
                s_sclr[0:nvals, 0:1], p_tf[0][0:nvals, 0:128], axis=AX),
              reads=["ptf0"], writes=["sclrA"])
            O("scalar", lambda e, nvals=nvals, ccm_slot=ccm_slot: e.dma_start(
                out=ccm_in[0, ccm_slot:ccm_slot + nvals],
                in_=s_sclr[0:nvals, 0:1]),
              reads=["sclrA", "ccmin_h"], writes=["ccmin_h"], dma="d_sp")
            O("gpsimd", lambda e: e.collective_compute(
                "AllReduce", OP.max, replica_groups=[list(range(cfg.NC))],
                ins=[ccm_in.opt()], outs=[ccm_out.opt()]),
              reads=["ccmin_h"], writes=["ccmout_h"])
            O("sync", lambda e: e.dma_start(
                out=s_sclr[0:1, 0:8], in_=ccm_out[0:1, :]),
              reads=["ccmout_h", "sclr8"], writes=["sclr8"], dma="d_x")
            O("vector", lambda e, nvals=nvals, ccm_slot=ccm_slot:
              e.tensor_scalar(
                  s_sclr[0:1, 8:8 + nvals],
                  s_sclr[0:1, ccm_slot:ccm_slot + nvals],
                  RQMAX, 1e-12, OP.mult, OP.max),
              reads=["sclr8"], writes=["srow"])
            O("vector", lambda e, nvals=nvals: e.reciprocal(
                s_sclr[0:1, 8 + nvals:8 + 2 * nvals], s_sclr[0:1, 8:8 + nvals]),
              reads=["srow"], writes=["irow"])
            O("vector", lambda e, nvals=nvals: e.tensor_scalar_mul(
                s_sclr[0:1, 8 + 2 * nvals:8 + 3 * nvals],
                s_sclr[0:1, 8:8 + nvals], -32768.0),
              reads=["srow"], writes=["brow"])
            O("tensor", lambda e, nvals=nvals: e.matmul(
                p_tf[1][:, 0:3 * nvals], s_ones32[:],
                s_sclr[0:1, 8:8 + 3 * nvals], start=True, stop=True),
              reads=["srow", "irow", "brow", "ones32"], writes=["ptf1"])
            O("scalar", lambda e, nvals=nvals, scl_slot=scl_slot: e.copy(
                s_scl[:, scl_slot:scl_slot + 3 * nvals],
                p_tf[1][:, 0:3 * nvals]),
              reads=["ptf1"], writes=["scl"])

        O("vector", lambda e: e.tensor_copy(s_m1[:, 0:1], s_xmax[:]),
          reads=["xmax"], writes=["m1"])
        colmax_to_scales(1, 0, 0, 1)
        SC_SX, SC_IX, SC_BX = 0, 1, 2
        if stop == "scl":
            ffnA0.close(); ffn.close(); _end(); return nc

        ntile = LC // 128
        for w in range(ntile):
            sl = w % 2
            f1s = s_f1[:, 1024 * sl:1024 * (sl + 1)]
            i32s = s_i32[:, 1024 * sl:1024 * (sl + 1)]
            fbfs = s_fbf[:, 1024 * sl:1024 * (sl + 1)]
            O("sync", lambda e, w=w, f1s=f1s: e.dma_start(
                out=f1s, in_=h_x1n[128 * w:128 * (w + 1), :]),
              reads=["h_x1n", "f1.%d" % sl], writes=["f1.%d" % sl], dma="d_r")
            O("scalar", lambda e, f1s=f1s, i32s=i32s: e.activation(
                i32s, f1s, AT.Identity,
                scale=s_scl[:, SC_IX:SC_IX + 1]),
              reads=["f1.%d" % sl, "scl", "i32.%d" % sl],
              writes=["i32.%d" % sl])
            O("scalar", lambda e, i32s=i32s, fbfs=fbfs: e.activation(
                fbfs, i32s, AT.Identity,
                scale=s_scl[:, SC_SX:SC_SX + 1]),
              reads=["i32.%d" % sl, "scl", "fbf.%d" % sl],
              writes=["fbf.%d" % sl])
            for kc in range(CM):
                pk = "ptr%d" % (kc % 2)
                O("tensor", lambda e, kc=kc, fbfs=fbfs: e.transpose(
                    p_tr[kc % 2][:, 0:128], fbfs[:, 128 * kc:128 * (kc + 1)],
                    s_idbf[:]),
                  reads=["fbf.%d" % sl, "idbf"], writes=[pk])
                O("scalar", lambda e, kc=kc, w=w, pk=pk: e.copy(
                    s_xq[:, LC * kc + 128 * w:LC * kc + 128 * (w + 1)],
                    p_tr[kc % 2][:, 0:128]),
                  reads=[pk], writes=["xq%d" % kc])

        if DBG_HBM:
            O("scalar", lambda e: e.dma_start(out=h_dbg_xq[:, :], in_=s_xq[:]),
              reads=[*("xq%d" % kc for kc in range(CM))], dma="d_sp")
        if stop == "xq":
            ffnA0.close(); ffn.close(); _end(); return nc
        P.dma_drain()
        P.barrier()
        ffnA0.close()
        s_stripG = sb(ffn, "s_stripG", (128, 24576), BF16)

        def mm_pass(h_wsrc, wkey, h_dst, dkey, maxcol, stb, spfx):
            parts = MM_PARTS
            if "strips" in parts:
                for k in range(8):
                    O("sync", lambda e, k=k, stb=stb: e.dma_start_transpose(
                        stb[:, 3072 * k:3072 * (k + 1)],
                        h_wsrc[:, 128 * k:128 * (k + 1)]),
                      reads=[wkey], writes=["%s%d" % (spfx, k)], dma="d_w")
            for tb2 in range(NTB // 2):
                for og in range(12):
                    for oi in range(2):
                        o = 2 * og + oi
                        pi = 2 * (og % 2) + oi
                        pm = "pmm%d" % pi
                        if "mm" in parts:
                            fr = []
                            for k in range(8):
                                fr.append((
                                    (lambda e, k=k, o=o, tb2=tb2, pi=pi,
                                     stb=stb: e.matmul(
                                        p_mm[pi][:, 0:512],
                                        stb[:, 3072 * k + 128 * o:
                                            3072 * k + 128 * (o + 1)],
                                        s_xq[:, LC * k + 512 * tb2:
                                             LC * k + 512 * (tb2 + 1)],
                                        start=(k == 0), stop=(k == 7))),
                                    ["%s%d" % (spfx, k), "xq%d" % k]))
                            P.mm_group(fr, writes=[pm])
                        fsl = (tb2 * 24 + o) % 4
                        f2s = s_f2[:, 512 * fsl:512 * (fsl + 1)]
                        if "evac" in parts:
                            O("scalar", lambda e, pi=pi, pm=pm, f2s=f2s: e.copy(
                                f2s, p_mm[pi][:, 0:512]),
                              reads=[pm, "f2.%d" % fsl], writes=["f2.%d" % fsl])
                        if "reduce" in parts:
                            O("vector", lambda e, f2s=f2s: e.tensor_reduce(
                                s_stat[:, 13:14], f2s, axis=AX,
                                op=OP.max, apply_absolute_value=True),
                              reads=["f2.%d" % fsl], writes=["ffm"])
                            O("vector", lambda e, maxcol=maxcol: e.tensor_max(
                                s_m1[:, maxcol:maxcol + 1],
                                s_m1[:, maxcol:maxcol + 1], s_stat[:, 13:14]),
                              reads=["ffm", "m1"], writes=["m1"])
                        if "spill" in parts:
                            O("scalar", lambda e, o=o, tb2=tb2, f2s=f2s:
                              e.dma_start(
                                out=h_dst[128 * o:128 * (o + 1),
                                          512 * tb2:512 * (tb2 + 1)],
                                in_=f2s),
                              reads=["f2.%d" % fsl],
                              writes=["%s.%d.%d" % (dkey, o, 2 * tb2),
                                      "%s.%d.%d" % (dkey, o, 2 * tb2 + 1)],
                              dma="d_sp")

        O("vector", lambda e: e.memset(s_m1[:, 0:4], 0.0),
          reads=["m1"], writes=["m1"])
        mm_pass(h_wu, "h_wu", h_up, "h_up", 0, s_strip, "strip")
        if stop == "up":
            ffn.close(); _end(); return nc
        mm_pass(h_wg, "h_wg", h_gate, "h_gate", 1, s_stripG, "stripG")

        P.dma_drain()
        P.barrier()
        ffn.close()
        ffn = ExitStack()
        NSL = 8
        s_f1B = sb(ffn, "s_f1b", (128, NSL * 1024), F32)
        s_f2B = sb(ffn, "s_f2b", (128, NSL * 1024), F32)
        s_f3B = sb(ffn, "s_f3b", (128, NSL * 1024), F32)
        s_i32B = sb(ffn, "s_i32b", (128, NSL * 1024), I32)

        colmax_to_scales(2, 1, 3, 2)
        SC_SU, SC_SG, SC_IU, SC_IG, SC_BU, SC_BG = 3, 4, 5, 6, 7, 8

        ntile2 = CI * (LC // 1024)
        ncl = LC // 1024
        O("vector", lambda e: e.memset(s_m1[:, 0:1], 0.0),
          reads=["m1"], writes=["m1"])
        for w in range(ntile2):
            oi_ = w % CI
            cl_i = w // CI
            r = oi_ * 128
            cl = cl_i * 1024
            tb0 = 4 * cl_i
            sl = w % NSL
            f1s = s_f1B[:, 1024 * sl:1024 * (sl + 1)]
            f2s = s_f2B[:, 1024 * sl:1024 * (sl + 1)]
            f3s = s_f3B[:, 1024 * sl:1024 * (sl + 1)]
            i32s = s_i32B[:, 1024 * sl:1024 * (sl + 1)]
            k1, k2, k3, ki = ("f1.%d" % sl, "f2.%d" % sl, "f3.%d" % sl,
                              "i32.%d" % sl)
            O("sync", lambda e, r=r, cl=cl, f1s=f1s: e.dma_start(
                out=f1s, in_=h_gate[r:r + 128, cl:cl + 1024]),
              reads=[k1] + ["h_gate.%d.%d" % (oi_, tb0 + j)
                            for j in range(4)],
              writes=[k1], dma="d_r")
            O("scalar", lambda e, f1s=f1s, i32s=i32s: e.activation(
                i32s, f1s, AT.Identity, scale=s_scl[:, SC_IG:SC_IG + 1]),
              reads=[k1, "scl", ki], writes=[ki])
            O("scalar", lambda e, f1s=f1s, i32s=i32s: e.activation(
                f1s, i32s, AT.Identity, scale=s_scl[:, SC_SG:SC_SG + 1]),
              reads=[ki, "scl", k1], writes=[k1])
            O("scalar", lambda e, f1s=f1s, f2s=f2s: e.activation(
                f2s, f1s, AT.Sigmoid),
              reads=[k1, k2], writes=[k2])
            O("scalar", lambda e, f2s=f2s, i32s=i32s: e.activation(
                i32s, f2s, AT.Identity, scale=65536.0),
              reads=[k2, ki], writes=[ki])
            O("vector", lambda e, f2s=f2s, i32s=i32s: e.tensor_scalar(
                f2s, i32s, 65535.0, R65536, OP.min, OP.mult),
              reads=[ki, k2], writes=[k2])
            O("vector", lambda e, f1s=f1s, f2s=f2s, f3s=f3s: e.tensor_mul(
                f3s, f1s, f2s),
              reads=[k1, k2, k3], writes=[k3])
            O("vector", lambda e, f3s=f3s: e.tensor_reduce(
                s_stat[:, 13:14], f3s, axis=AX, op=OP.max,
                apply_absolute_value=True),
              reads=[k3], writes=["ffm"])
            O("vector", lambda e: e.tensor_max(
                s_m1[:, 0:1], s_m1[:, 0:1], s_stat[:, 13:14]),
              reads=["ffm", "m1"], writes=["m1"])
            O("scalar", lambda e, r=r, cl=cl, f3s=f3s: e.dma_start(
                out=h_g2[r:r + 128, cl:cl + 1024], in_=f3s),
              reads=[k3],
              writes=["h_g2.%d.%d" % (oi_, tb0 + j) for j in range(4)],
              dma="d_sp")
        colmax_to_scales(1, 3, 9, 1)
        SC_S2, SC_I2, SC_B2 = 9, 10, 11
        if stop == "g2":
            ffn.close(); _end(); return nc

        O("vector", lambda e: e.memset(s_m1[:, 0:1], 0.0),
          reads=["m1"], writes=["m1"])
        for w in range(ntile2):
            oi_ = w % CI
            cl_i = w // CI
            r = oi_ * 128
            cl = cl_i * 1024
            tb0 = 4 * cl_i
            sl = w % NSL
            f1s = s_f1B[:, 1024 * sl:1024 * (sl + 1)]
            f2s = s_f2B[:, 1024 * sl:1024 * (sl + 1)]
            f3s = s_f3B[:, 1024 * sl:1024 * (sl + 1)]
            i32s = s_i32B[:, 1024 * sl:1024 * (sl + 1)]
            k1, k2, k3, ki = ("f1.%d" % sl, "f2.%d" % sl, "f3.%d" % sl,
                              "i32.%d" % sl)
            O("sync", lambda e, r=r, cl=cl, f1s=f1s: e.dma_start(
                out=f1s, in_=h_g2[r:r + 128, cl:cl + 1024]),
              reads=[k1] + ["h_g2.%d.%d" % (oi_, tb0 + j) for j in range(4)],
              writes=[k1], dma="d_r")
            O("sync", lambda e, r=r, cl=cl, f2s=f2s: e.dma_start(
                out=f2s, in_=h_up[r:r + 128, cl:cl + 1024]),
              reads=[k2] + ["h_up.%d.%d" % (oi_, tb0 + j) for j in range(4)],
              writes=[k2], dma="d_r")
            O("scalar", lambda e, f1s=f1s, i32s=i32s: e.activation(
                i32s, f1s, AT.Identity, scale=s_scl[:, SC_I2:SC_I2 + 1]),
              reads=[k1, "scl", ki], writes=[ki])
            O("scalar", lambda e, f1s=f1s, i32s=i32s: e.activation(
                f1s, i32s, AT.Identity, scale=s_scl[:, SC_S2:SC_S2 + 1]),
              reads=[ki, "scl", k1], writes=[k1])
            O("scalar", lambda e, f2s=f2s, i32s=i32s: e.activation(
                i32s, f2s, AT.Identity, scale=s_scl[:, SC_IU:SC_IU + 1]),
              reads=[k2, "scl", ki], writes=[ki])
            O("scalar", lambda e, f2s=f2s, i32s=i32s: e.activation(
                f2s, i32s, AT.Identity, scale=s_scl[:, SC_SU:SC_SU + 1]),
              reads=[ki, "scl", k2], writes=[k2])
            O("vector", lambda e, f1s=f1s, f2s=f2s, f3s=f3s: e.tensor_mul(
                f3s, f1s, f2s),
              reads=[k1, k2, k3], writes=[k3])
            O("vector", lambda e, f3s=f3s: e.tensor_reduce(
                s_stat[:, 13:14], f3s, axis=AX, op=OP.max,
                apply_absolute_value=True),
              reads=[k3], writes=["ffm"])
            O("vector", lambda e: e.tensor_max(
                s_m1[:, 0:1], s_m1[:, 0:1], s_stat[:, 13:14]),
              reads=["ffm", "m1"], writes=["m1"])
            O("scalar", lambda e, r=r, cl=cl, f3s=f3s: e.dma_start(
                out=h_gate[r:r + 128, cl:cl + 1024], in_=f3s),
              reads=[k3],
              writes=["h_o.%d.%d" % (oi_, tb0 + j) for j in range(4)]
              + ["h_gate.%d.%d" % (oi_, tb0 + j) for j in range(4)],
              dma="d_sp")
        P.dma_drain()
        P.barrier()
        ffn.close()
        ffn = ExitStack()
        s_stripC = sb(ffn, "s_strip2", (128, 24576), BF16)
        s_f1C = sb(ffn, "s_f1c", (128, 4 * 512), F32)
        s_i32C = sb(ffn, "s_i32c", (128, 4 * 512), I32)
        s_f3C = sb(ffn, "s_f3c", (128, 2 * 1024), F32)
        s_obf = sb(ffn, "s_obf", (128, CI * 512), BF16)
        s_dfm = sb(ffn, "s_dfm", (128, CM * 512), F32)
        s_x2 = sb(ffn, "s_x2", (128, 2 * DM), F32)
        s_sq2 = sb(ffn, "s_sq2", (128, DM), F32)

        colmax_to_scales(1, 4, 12, 1)
        SC_SO, SC_IO, SC_BO = 12, 13, 14
        if stop == "o":
            ffn.close(); _end(); return nc

        for k in range(CI):
            O("sync", lambda e, k=k: e.dma_start_transpose(
                s_stripC[:, 1024 * k:1024 * (k + 1)],
                h_wd[:, 128 * k:128 * (k + 1)]),
              reads=["h_wd"], writes=["dstrip%d" % k], dma="d_w")
        last_out = [None]
        for tb2 in range(NTB // 2):
            for k in range(CI):
                sl = k % 4
                f1s = s_f1C[:, 512 * sl:512 * (sl + 1)]
                i32s = s_i32C[:, 512 * sl:512 * (sl + 1)]
                k1, ki = "f1.%d" % sl, "i32.%d" % sl
                O("sync", lambda e, k=k, tb2=tb2, f1s=f1s: e.dma_start(
                    out=f1s,
                    in_=h_gate[128 * k:128 * (k + 1),
                               512 * tb2:512 * (tb2 + 1)]),
                  reads=["h_o.%d.%d" % (k, 2 * tb2),
                         "h_o.%d.%d" % (k, 2 * tb2 + 1), k1],
                  writes=[k1], dma="d_r")
                O("scalar", lambda e, f1s=f1s, i32s=i32s: e.activation(
                    i32s, f1s, AT.Identity,
                    scale=s_scl[:, SC_IO:SC_IO + 1]),
                  reads=[k1, "scl", ki], writes=[ki])
                O("scalar", lambda e, k=k, i32s=i32s: e.activation(
                    s_obf[:, 512 * k:512 * (k + 1)], i32s,
                    AT.Identity, scale=s_scl[:, SC_SO:SC_SO + 1]),
                  reads=[ki, "scl"], writes=["ob%d" % k])
            for m in range(CM):
                pm = "pmm%d" % (m % 2)
                fr = []
                for k in range(CI):
                    fr.append((
                        (lambda e, k=k, m=m: e.matmul(
                            p_mm[m % 2][:, 0:512],
                            s_stripC[:, 1024 * k + 128 * m:
                                    1024 * k + 128 * (m + 1)],
                            s_obf[:, 512 * k:512 * (k + 1)],
                            start=(k == 0), stop=(k == CI - 1))),
                        ["dstrip%d" % k, "ob%d" % k]))
                P.mm_group(fr, writes=[pm])
                O("scalar", lambda e, m=m, pm=pm: e.copy(
                    s_dfm[:, 512 * m:512 * (m + 1)], p_mm[m % 2][:, 0:512]),
                  reads=[pm], writes=["dfm%d" % m])
            for i in range(4):
                sl = i % 2
                tb = 2 * tb2 + i // 2
                f3s = s_f3C[:, 1024 * sl:1024 * (sl + 1)]
                k3 = "f3.%d" % sl
                O("sync", lambda e, tb2=tb2, i=i, f3s=f3s: e.dma_start(
                    out=f3s, in_=h_x1n[512 * tb2 + 128 * i:
                                       512 * tb2 + 128 * (i + 1), :]),
                  reads=["h_x1n", k3], writes=[k3], dma="d_x")
                for m in range(CM):
                    pf = "ptf%d" % (m % 2)
                    O("tensor", lambda e, m=m, i=i: e.transpose(
                        p_tf[m % 2][:],
                        s_dfm[:, 512 * m + 128 * i:512 * m + 128 * (i + 1)],
                        s_id32[:]),
                      reads=["dfm%d" % m, "id32"], writes=[pf])
                    O("vector", lambda e, m=m, sl=sl, pf=pf, f3s=f3s:
                      e.tensor_add(
                          s_x2[:, DM * sl + 128 * m:DM * sl + 128 * (m + 1)],
                          p_tf[m % 2][:], f3s[:, 128 * m:128 * (m + 1)]),
                      reads=[pf, k3], writes=["x2t%d" % sl])
                x2t = s_x2[:, DM * sl:DM * (sl + 1)]
                O("scalar", lambda e, x2t=x2t: e.activation(
                    s_sq2[:], x2t, AT.Square, accum_out=s_stat[:, 3:4]),
                  reads=["x2t%d" % sl, "sq2"], writes=["lst0", "sq2"])
                O("vector", lambda e, x2t=x2t: e.reduce_sum(
                    s_stat[:, 4:5], x2t, axis=AX),
                  reads=["x2t%d" % sl], writes=["lst1"])
                O("vector", lambda e: e.tensor_scalar_mul(
                    s_stat[:, 5:6], s_stat[:, 4:5], 1.0 / DM),
                  reads=["lst1"], writes=["lmean"])
                O("vector", lambda e: e.tensor_mul(
                    s_stat[:, 6:7], s_stat[:, 5:6], s_stat[:, 5:6]),
                  reads=["lmean"], writes=["lm2"])
                O("vector", lambda e: e.scalar_tensor_tensor(
                    s_stat[:, 7:8], s_stat[:, 3:4], 1.0 / DM,
                    s_stat[:, 6:7], OP.mult, OP.subtract),
                  reads=["lst0", "lm2"], writes=["lvar"])
                O("scalar", lambda e: e.activation(
                    s_stat[:, 9:10], s_stat[:, 7:8], AT.Sqrt, bias=EPS),
                  reads=["lvar", "kconst"], writes=["lsd"])
                O("vector", lambda e: e.reciprocal(
                    s_stat[:, 10:11], s_stat[:, 9:10]),
                  reads=["lsd"], writes=["lrs"])
                O("vector", lambda e: e.tensor_scalar_mul(
                    s_stat[:, 11:12], s_stat[:, 5:6], -1.0),
                  reads=["lmean"], writes=["lnm"])
                O("vector", lambda e, x2t=x2t: e.tensor_scalar(
                    x2t, x2t, s_stat[:, 11:12], s_stat[:, 10:11],
                    OP.add, OP.mult),
                  reads=["x2t%d" % sl, "lnm", "lrs"], writes=["x2t%d" % sl])
                last_out[0] = O("scalar", lambda e, tb2=tb2, i=i,
                                x2t=x2t: e.dma_start(
                                    out=out[512 * tb2 + 128 * i:
                                            512 * tb2 + 128 * (i + 1), :],
                                    in_=x2t),
                                reads=["x2t%d" % sl], writes=["outw"],
                                dma="d_o")
        P._dep_wait("sync", last_out[0])
        P.op("sync", lambda e: e.nop())

        ffn.close()
        _end()

    return nc



def make_inmaps(cfg, inputs):
    x = np.ascontiguousarray(np.asarray(inputs["L0_attn_x"], np.float32))
    B, L, _ = x.shape
    xf = x.reshape(B * L, DM)
    lc = cfg.LC
    conv_w = np.asarray(inputs["L0_attn_conv_w"], np.float32)[:, 0, :]
    conv_b = np.asarray(inputs["L0_attn_conv_b"], np.float32)
    Dp = np.asarray(inputs["L0_attn_D"], np.float32)
    dt_b = np.asarray(inputs["L0_attn_dt_proj_b"], np.float32)
    A_log = np.asarray(inputs["L0_attn_A_log"], np.float32)
    norm_w = np.asarray(inputs["L0_attn_norm_w"], np.float32)

    chv = np.zeros((128, 7 * CD), np.float32)
    alog = np.zeros((128, CD * DS), np.float32)
    for c in range(CD):
        rows = slice(128 * c, 128 * (c + 1))
        chv[:, 7 * c:7 * c + 4] = conv_w[rows, :]
        chv[:, 7 * c + 4] = conv_b[rows]
        chv[:, 7 * c + 5] = Dp[rows]
        chv[:, 7 * c + 6] = dt_b[rows]
        alog[:, DS * c:DS * (c + 1)] = A_log[rows, :]
    normw_rep = np.ascontiguousarray(
        np.broadcast_to(norm_w[None, :], (128, DM)))
    identity = np.eye(128, dtype=np.float32)
    ones_row = np.ones((1, 128), np.float32)

    common = dict(
        w_in=np.ascontiguousarray(np.asarray(inputs["L0_attn_in_proj_w"],
                                             np.float32)),
        w_out=np.ascontiguousarray(np.asarray(inputs["L0_attn_out_proj_w"],
                                              np.float32)),
        w_xp=np.ascontiguousarray(np.asarray(inputs["L0_attn_x_proj_w"],
                                             np.float32)),
        w_dt=np.ascontiguousarray(np.asarray(inputs["L0_attn_dt_proj_w"],
                                             np.float32)),
        w_g=np.ascontiguousarray(np.asarray(inputs["L0_ffn_w_gate"],
                                            np.float32)),
        w_u=np.ascontiguousarray(np.asarray(inputs["L0_ffn_w_up"],
                                            np.float32)),
        w_d=np.ascontiguousarray(np.asarray(inputs["L0_ffn_w_down"],
                                            np.float32)),
        chvec=chv, alog=alog, normw=normw_rep, ident=identity,
        ones_r=ones_row,
    )
    in_maps = []
    cores_per_batch = cfg.NC // B if cfg.NC >= B else 1
    for core in range(cfg.NC):
        start = core * lc
        first = (core % cores_per_batch) == 0
        if first:
            halo = np.zeros((3, DM), np.float32)
        else:
            halo = xf[start - 3:start]
        g0 = (core // cores_per_batch) * cores_per_batch
        chain = np.zeros((128, 16), np.float32)
        for j in range(cfg.NC):
            if g0 <= j < core:
                chain[:, 2 * j] = 1.0
                chain[:, 2 * j + 1] = 0.0
            else:
                chain[:, 2 * j] = 0.0
                chain[:, 2 * j + 1] = 1.0
        m = dict(common)
        m["x_in"] = np.ascontiguousarray(xf[start:start + lc])
        m["xh3"] = np.ascontiguousarray(halo)
        m["chain_ab"] = chain
        in_maps.append(m)
    return in_maps


_CFG = CFG()
_NC_CACHE = {}


def kernel(**inputs):
    cfg = _CFG
    key = (cfg.LC, cfg.T, cfg.GP, cfg.NC)
    if key not in _NC_CACHE:
        _NC_CACHE[key] = build_nc(cfg)
    nc = _NC_CACHE[key]
    in_maps = make_inmaps(cfg, inputs)
    res = run_bass_kernel_spmd(nc, in_maps, core_ids=list(range(cfg.NC)))
    outs = [res.results[c]["out"] for c in range(cfg.NC)]
    full = np.concatenate(outs, axis=0)
    x = np.asarray(inputs["L0_attn_x"])
    return full.reshape(x.shape).astype(np.float32)
